# version 06
# baseline (speedup 1.0000x reference)
import numpy as np
import concourse.bass as bass
import concourse.mybir as mybir
from concourse.bass_utils import run_bass_kernel_spmd
from contextlib import ExitStack

F32 = mybir.dt.float32
BF16 = mybir.dt.bfloat16
FP16 = mybir.dt.float16
I8 = mybir.dt.int8
I32 = mybir.dt.int32
AF = mybir.ActivationFunctionType
ALU = mybir.AluOpType
AXX = mybir.AxisListType.X

T = 2048
D = 4096
KVD = 2048
NT = T // 128
NK = D // 128
N_QK = 48
NVB = KVD // 256
NFB = D // 256
NCQ = 4
SCALE = float(1.0 / np.sqrt(np.float32(32.0)))
SHIFT = 12.0
MAGIC = float(1.5 * 2.0 ** 23)
EPS = 1e-5


def build_nc(phases=4):
    nc = bass.Bass("TRN2", target_bir_lowering=False)

    x = nc.dram_tensor("x", [T, D], F32, kind="ExternalInput").ap()
    wq = nc.dram_tensor("wq", [32, 128, NK * 128], F32, kind="ExternalInput").ap()
    wk = nc.dram_tensor("wk", [16, 128, NK * 128], F32, kind="ExternalInput").ap()
    wv = nc.dram_tensor("wv", [NVB, 128, NK * 256], F32, kind="ExternalInput").ap()
    wf = nc.dram_tensor("wf", [NFB, 128, NK * 256], I8, kind="ExternalInput").ap()
    bq = nc.dram_tensor("bq", [128, 32], F32, kind="ExternalInput").ap()
    bk = nc.dram_tensor("bk", [128, 16], F32, kind="ExternalInput").ap()
    bv_rep = nc.dram_tensor("bv_rep", [128, KVD], F32, kind="ExternalInput").ap()
    ws_rep = nc.dram_tensor("ws_rep", [128, D], F32, kind="ExternalInput").ap()
    mos_rep = nc.dram_tensor("mos_rep", [128, 1], F32, kind="ExternalInput").ap()
    out = nc.dram_tensor("out", [T, D], F32, kind="ExternalOutput").ap()

    s_xnT = nc.dram_tensor("s_xnT", [128, NT * NK * 128], BF16, kind="Internal").ap()
    s_xqT = nc.dram_tensor("s_xqT", [128, NT * NK * 128], BF16, kind="Internal").ap()
    s_qT = nc.dram_tensor("s_qT", [D, T], BF16, kind="Internal").ap()
    s_kT = nc.dram_tensor("s_kT", [KVD, T], BF16, kind="Internal").ap()
    s_v = nc.dram_tensor("s_v", [T, KVD], BF16, kind="Internal").ap()
    s_par = nc.dram_tensor("s_par", [T, D], F32, kind="Internal").ap()
    xnT3 = s_xnT.rearrange("p (tt kc m) -> p tt kc m", tt=NT, kc=NK)
    xqT3 = s_xqT.rearrange("p (tt kc m) -> p tt kc m", tt=NT, kc=NK)

    with ExitStack() as es:
        ec = es.enter_context
        cpool = ec(nc.sbuf_tensor("cpool", [128, 4], F32))
        mos_sb = ec(nc.sbuf_tensor("mos_sb", [128, 1], F32))
        xs_all = ec(nc.sbuf_tensor("xs_all", [128, NT], F32))
        iota32 = ec(nc.sbuf_tensor("iota32", [128, 128], I32))
        mask128 = ec(nc.sbuf_tensor("mask128", [128, 128], BF16))
        ident = ec(nc.sbuf_tensor("ident", [128, 128], BF16))

        sem = {}
        for name in ["mos",
                     "x1s0", "x1s1",
                     "a1", "d1", "p1", "e1",
                     "st1s0", "st1s1", "st1s2", "st1s3",
                     "w2s0", "w2s1", "l2", "m2", "e2",
                     "st2s0", "st2s1",
                     "w2vs0", "w2vs1", "m2v", "e2v",
                     "st2vs0", "st2vs1", "st2vs2", "st2vs3",
                     "w3s0", "w3s1", "l3",
                     "x3s0", "x3s1", "m3", "a3", "e3",
                     "st3s0", "st3s1",
                     "pt4", "kt4s0", "kt4s1", "v4s0", "v4s1", "qt4s0", "qt4s1",
                     "m4", "a4", "d4", "so"]:
            sem[name] = ec(nc.semaphore("s_" + name))

        block = ec(nc.Block())

        ST1_DONE = [("st1s%d" % j, 16 * 8) for j in range(4)]
        ST2_DONE = [("st2s%d" % j, 16 * 24) for j in range(2)]
        ST2V_DONE = [("st2vs%d" % j, 16 * 32) for j in range(4)]
        ST3_DONE = [("st3s%d" % j, 16 * 128) for j in range(2)]

        def wait_all(eng, pairs):
            for name, val in pairs:
                eng.wait_ge(sem[name], val)

        def drain(pairs):
            @block.sync
            def _(sync):
                wait_all(sync, pairs)

        @block.gpsimd
        def _(g):
            g.iota(iota32[:], pattern=[[-1, 128]], base=0, channel_multiplier=1).then_inc(sem["d1"], 1)

        @block.sync
        def _(sync):
            sync.dma_start(out=mos_sb[:], in_=mos_rep[:]).then_inc(sem["mos"], 16)

        @block.vector
        def _(v):
            v.wait_ge(sem["d1"], 1)
            v.memset(cpool[:, 0:1], SCALE)
            v.memset(cpool[:, 1:2], -SHIFT)
            v.memset(cpool[:, 2:3], EPS)
            v.tensor_single_scalar(mask128[:], iota32[:], 0.0, ALU.is_le)
            v.tensor_single_scalar(ident[:], iota32[:], 0.0, ALU.is_equal).then_inc(sem["d1"], 1)

        SET_D1 = 2
        DP = 16

        def d1(i, k):
            return SET_D1 + i * DP + k

        with ExitStack() as p1s:
            pc1 = p1s.enter_context
            x_t = [pc1(nc.sbuf_tensor(f"x_t{i}", [128, D], F32)) for i in range(2)]
            xn_bf = [pc1(nc.sbuf_tensor(f"xn_bf{i}", [128, D], BF16)) for i in range(2)]
            xq_bf = [pc1(nc.sbuf_tensor(f"xq_bf{i}", [128, D], BF16)) for i in range(2)]
            qf = pc1(nc.sbuf_tensor("qf", [128, D], F32))
            qi8 = [pc1(nc.sbuf_tensor(f"qi8_{i}", [128, D], I8)) for i in range(2)]
            jnk_bf = pc1(nc.sbuf_tensor("jnk_bf", [128, D], BF16))
            stat = pc1(nc.sbuf_tensor("stat", [128, 2, 10], F32))
            xs16 = pc1(nc.sbuf_tensor("xs16", [128, 2], FP16))
            strip = [pc1(nc.sbuf_tensor(f"strip{i}", [128, NK, 128], BF16)) for i in range(4)]
            tp_ps = [pc1(nc.psum_tensor(f"tp_ps{i}", [128, 1024], BF16)) for i in range(2)]

            def emit_strip_store(sync, i):
                for half in range(2):
                    slot = 2 * (i % 2) + half
                    sync.wait_ge(sem["e1"], 8 * i + 4 * half + 4)
                    dst = xnT3 if half == 0 else xqT3
                    sync.dma_start(out=dst[:, i, :, :],
                                   in_=strip[slot][:]).then_inc(sem["st1s%d" % slot], 16)

            @block.sync
            def _(sync):
                for i in range(NT):
                    if i >= 2:
                        sync.wait_ge(sem["d1"], d1(i - 2, 13))
                        sync.wait_ge(sem["a1"], 4 * (i - 2) + 2)
                    sync.dma_start(out=x_t[i % 2][:],
                                   in_=x[i * 128:(i + 1) * 128, :]).then_inc(sem["x1s%d" % (i % 2)], 16)
                    if i >= 2:
                        emit_strip_store(sync, i - 2)
                for i in range(NT - 2, NT):
                    emit_strip_store(sync, i)

            @block.scalar
            def _(s):
                for i in range(NT):
                    p = i % 2
                    s.wait_ge(sem["x1s%d" % p], 16 * (i // 2 + 1))
                    s.activation(jnk_bf[:], x_t[p][:], AF.Copy, accum_out=stat[:, p, 0:1])
                    s.activation(jnk_bf[:], x_t[p][:], AF.Square, accum_out=stat[:, p, 1:2]).then_inc(sem["a1"], 2)
                    s.wait_ge(sem["d1"], d1(i, 4))
                    s.activation(stat[:, p, 5:6], stat[:, p, 4:5], AF.Sqrt, bias=cpool[:, 2:3]).then_inc(sem["a1"], 1)
                    s.wait_ge(sem["d1"], d1(i, 16))
                    if i >= 2:
                        s.wait_ge(sem["p1"], 8 * (i - 2) + 8)
                    s.copy(xq_bf[p][:], qi8[p][:]).then_inc(sem["a1"], 1)

            @block.vector
            def _(v):
                dcnt = [SET_D1]

                def op(ins):
                    ins.then_inc(sem["d1"], 1)
                    dcnt[0] += 1

                def sw():
                    v.wait_ge(sem["d1"], dcnt[0])

                for i in range(NT):
                    p = i % 2
                    c = stat[:, p, :]
                    assert dcnt[0] == d1(i, 0)
                    v.wait_ge(sem["a1"], 4 * i + 2)
                    op(v.tensor_scalar(c[:, 2:3], c[:, 0:1], 1.0 / D, None, ALU.mult))
                    op(v.tensor_scalar(c[:, 3:4], c[:, 1:2], 1.0 / D, None, ALU.mult))
                    sw()
                    op(v.tensor_tensor(c[:, 4:5], c[:, 2:3], c[:, 2:3], ALU.mult))
                    sw()
                    op(v.tensor_tensor(c[:, 4:5], c[:, 3:4], c[:, 4:5], ALU.subtract))
                    v.wait_ge(sem["a1"], 4 * i + 3)
                    op(v.reciprocal(c[:, 6:7], c[:, 5:6]))
                    if i >= 2:
                        v.wait_ge(sem["p1"], 8 * (i - 2) + 4)
                    sw()
                    op(v.tensor_scalar(xn_bf[p][:], x_t[p][:], c[:, 2:3], c[:, 6:7],
                                       ALU.subtract, ALU.mult))
                    op(v.reduce_max(c[:, 7:8], xn_bf[p][:], AXX, apply_absolute_value=True))
                    sw()
                    op(v.tensor_scalar(c[:, 7:8], c[:, 7:8], 1.0 / 127.0, None, ALU.mult))
                    sw()
                    op(v.tensor_copy(xs16[:, p:p + 1], c[:, 7:8]))
                    sw()
                    op(v.tensor_copy(xs_all[:, i:i + 1], xs16[:, p:p + 1]))
                    op(v.reciprocal(c[:, 8:9], c[:, 7:8]))
                    sw()
                    op(v.tensor_tensor(c[:, 9:10], c[:, 6:7], c[:, 8:9], ALU.mult))
                    sw()
                    op(v.tensor_scalar(qf[:], x_t[p][:], c[:, 2:3], c[:, 9:10],
                                       ALU.subtract, ALU.mult))
                    op(v.tensor_scalar(qf[:], qf[:], MAGIC, None, ALU.add))
                    op(v.tensor_scalar(qf[:], qf[:], -MAGIC, None, ALU.add))
                    op(v.tensor_copy(qi8[p][:], qf[:]))
                    assert dcnt[0] == d1(i, DP)
                    for half in range(2):
                        slot = 2 * p + half
                        for bg in range(4):
                            g_ = 8 * i + 4 * half + bg
                            v.wait_ge(sem["p1"], g_ + 1)
                            if i >= 2 and bg == 0:
                                v.wait_ge(sem["st1s%d" % slot], 16 * (i // 2))
                            v.tensor_copy(strip[slot][:, 8 * bg:8 * bg + 8, :],
                                          tp_ps[g_ % 2][:]).then_inc(sem["e1"], 1)

            @block.tensor
            def _(t):
                for i in range(NT):
                    p = i % 2
                    for half in range(2):
                        src = xn_bf[p] if half == 0 else xq_bf[p]
                        if half == 0:
                            t.wait_ge(sem["d1"], d1(i, 6))
                        else:
                            t.wait_ge(sem["a1"], 4 * i + 4)
                        for bg in range(4):
                            g_ = 8 * i + 4 * half + bg
                            if g_ >= 2:
                                t.wait_ge(sem["e1"], g_ - 1)
                            for j in range(8):
                                col = 8 * bg + j
                                ins = t.matmul(tp_ps[g_ % 2][:, j * 128:(j + 1) * 128],
                                               src[:, col * 128:(col + 1) * 128], ident[:],
                                               is_transpose=True, start=(j == 0), stop=(j == 7))
                            ins.then_inc(sem["p1"], 1)

        if phases < 2:
            drain(ST1_DONE)
            return nc

        with ExitStack() as p2s:
            pc2 = p2s.enter_context
            xnT_sb = pc2(nc.sbuf_tensor("xnT_sb", [128, NT, NK, 128], BF16))
            bq_sb = pc2(nc.sbuf_tensor("bq_sb", [128, 32], F32))
            bk_sb = pc2(nc.sbuf_tensor("bk_sb", [128, 16], F32))
            bv_sb = pc2(nc.sbuf_tensor("bv_sb", [128, KVD], F32))
            wqk_blk = [pc2(nc.sbuf_tensor(f"wqk{i}", [128, NK, 128], BF16)) for i in range(2)]
            qk_stage = [pc2(nc.sbuf_tensor(f"qks{i}", [128, T], BF16)) for i in range(2)]
            wv_blk = [pc2(nc.sbuf_tensor(f"wvb{i}", [128, NK, 256], BF16)) for i in range(2)]
            v_stage = [pc2(nc.sbuf_tensor(f"vst{i}", [128, 256], BF16)) for i in range(4)]

            @block.sync
            def _(sync):
                wait_all(sync, ST1_DONE)
                for q4 in range(4):
                    sync.dma_start(out=xnT_sb[:, q4 * 4:(q4 + 1) * 4, :, :],
                                   in_=xnT3[:, q4 * 4:(q4 + 1) * 4, :, :]).then_inc(sem["l2"], 16)
                sync.dma_start(out=bq_sb[:], in_=bq[:]).then_inc(sem["l2"], 16)
                sync.dma_start(out=bk_sb[:], in_=bk[:]).then_inc(sem["l2"], 16)
                sync.dma_start(out=bv_sb[:], in_=bv_rep[:]).then_inc(sem["l2"], 16)
            L2_TOTAL = 7 * 16

            @block.gpsimd
            def _(g):
                wait_all(g, ST1_DONE)
                for nt in range(N_QK):
                    if nt >= 2:
                        g.wait_ge(sem["m2"], 4 * (nt - 2) + 4)
                    src = wq[nt] if nt < 32 else wk[nt - 32]
                    g.dma_start(out=wqk_blk[nt % 2][:],
                                in_=src.rearrange("p (kc m) -> p kc m", kc=NK)
                                ).then_inc(sem["w2s%d" % (nt % 2)], 16)
                for vb in range(NVB):
                    if vb >= 2:
                        g.wait_ge(sem["m2v"], 16 * (vb - 1))
                    g.dma_start(out=wv_blk[vb % 2][:],
                                in_=wv[vb].rearrange("p (kc m) -> p kc m", kc=NK)
                                ).then_inc(sem["w2vs%d" % (vb % 2)], 16)

            with ExitStack() as p2qk:
                qk_ps = [p2qk.enter_context(nc.psum_tensor(f"qk_ps{i}", [128, T], F32)) for i in range(2)]

                @block.tensor
                def _(t):
                    t.wait_ge(sem["l2"], L2_TOTAL)
                    for nt in range(N_QK):
                        t.wait_ge(sem["w2s%d" % (nt % 2)], 16 * (nt // 2 + 1))
                        for tc in range(4):
                            if nt >= 2:
                                t.wait_ge(sem["e2"], 4 * (nt - 2) + tc + 1)
                            for k in range(NK):
                                ins = t.matmul(qk_ps[nt % 2][:, tc * 512:(tc + 1) * 512],
                                               wqk_blk[nt % 2][:, k, :],
                                               xnT_sb[:, 4 * tc:4 * (tc + 1), k, :],
                                               start=(k == 0), stop=(k == NK - 1))
                            ins.then_inc(sem["m2"], 1)

                @block.vector
                def _(v):
                    for nt in range(N_QK):
                        for tc in range(4):
                            v.wait_ge(sem["m2"], 4 * nt + tc + 1)
                            if nt >= 2 and tc == 0:
                                v.wait_ge(sem["st2s%d" % (nt % 2)], 16 * (nt // 2))
                            bias = bq_sb[:, nt:nt + 1] if nt < 32 else bk_sb[:, nt - 32:nt - 31]
                            v.tensor_scalar(qk_stage[nt % 2][:, tc * 512:(tc + 1) * 512],
                                            qk_ps[nt % 2][:, tc * 512:(tc + 1) * 512],
                                            bias, None, ALU.add).then_inc(sem["e2"], 1)

                @block.sync
                def _(sync):
                    for nt in range(N_QK):
                        sync.wait_ge(sem["e2"], 4 * nt + 4)
                        dst = s_qT[nt * 128:(nt + 1) * 128, :] if nt < 32 else \
                            s_kT[(nt - 32) * 128:(nt - 31) * 128, :]
                        sync.dma_start(out=dst, in_=qk_stage[nt % 2][:]
                                       ).then_inc(sem["st2s%d" % (nt % 2)], 16)

            v_ps = [pc2(nc.psum_tensor(f"v_ps{i}", [128, 512], F32)) for i in range(4)]

            @block.tensor
            def _(t):
                t.wait_ge(sem["e2"], 4 * N_QK)
                for vb in range(NVB):
                    t.wait_ge(sem["w2vs%d" % (vb % 2)], 16 * (vb // 2 + 1))
                    for tt in range(NT):
                        j = vb * NT + tt
                        if j >= 4:
                            t.wait_ge(sem["e2v"], j - 3)
                        for k in range(NK):
                            ins = t.matmul(v_ps[j % 4][:, 0:256], xnT_sb[:, tt, k, :],
                                           wv_blk[vb % 2][:, k, :],
                                           start=(k == 0), stop=(k == NK - 1))
                        ins.then_inc(sem["m2v"], 1)

            @block.vector
            def _(v):
                for vb in range(NVB):
                    for tt in range(NT):
                        j = vb * NT + tt
                        v.wait_ge(sem["m2v"], j + 1)
                        if j >= 4:
                            v.wait_ge(sem["st2vs%d" % (j % 4)], 16 * (j // 4))
                        v.tensor_tensor(v_stage[j % 4][:], v_ps[j % 4][:, 0:256],
                                        bv_sb[:, vb * 256:(vb + 1) * 256], ALU.add).then_inc(sem["e2v"], 1)

            @block.sync
            def _(sync):
                for vb in range(NVB):
                    for tt in range(NT):
                        j = vb * NT + tt
                        sync.wait_ge(sem["e2v"], j + 1)
                        sync.dma_start(out=s_v[tt * 128:(tt + 1) * 128, vb * 256:(vb + 1) * 256],
                                       in_=v_stage[j % 4][:]).then_inc(sem["st2vs%d" % (j % 4)], 16)

        if phases < 3:
            drain(ST2_DONE + ST2V_DONE)
            return nc

        with ExitStack() as p3s:
            pc3 = p3s.enter_context
            xqT_sb = pc3(nc.sbuf_tensor("xqT_sb", [128, NT, NK, 128], BF16))
            ws_sb = pc3(nc.sbuf_tensor("ws_sb", [128, D], F32))
            wf_blk = [pc3(nc.sbuf_tensor(f"wfb{i}", [128, NK, 256], BF16)) for i in range(2)]
            y16 = [pc3(nc.sbuf_tensor(f"y16_{i}", [128, 256], FP16)) for i in range(2)]
            x_strip = [pc3(nc.sbuf_tensor(f"xs3_{i}", [128, 256], F32)) for i in range(2)]
            par_stage = [pc3(nc.sbuf_tensor(f"par{i}", [128, 256], F32)) for i in range(2)]
            f_ps = [pc3(nc.psum_tensor(f"f_ps{i}", [128, 512], F32)) for i in range(4)]

            NJ = NFB * NT

            @block.sync
            def _(sync):
                wait_all(sync, ST1_DONE + ST2_DONE + ST2V_DONE)
                for q4 in range(4):
                    sync.dma_start(out=xqT_sb[:, q4 * 4:(q4 + 1) * 4, :, :],
                                   in_=xqT3[:, q4 * 4:(q4 + 1) * 4, :, :]).then_inc(sem["l3"], 16)
                sync.dma_start(out=ws_sb[:], in_=ws_rep[:]).then_inc(sem["l3"], 16)
                for fb in range(NFB):
                    for tt in range(NT):
                        j = fb * NT + tt
                        if j >= 2:
                            sync.wait_ge(sem["e3"], 2 * (j - 2) + 2)
                        sync.dma_start(out=x_strip[j % 2][:],
                                       in_=x[tt * 128:(tt + 1) * 128, fb * 256:(fb + 1) * 256]
                                       ).then_inc(sem["x3s%d" % (j % 2)], 16)
                        if j >= 2:
                            jj = j - 2
                            fb2, tt2 = jj // NT, jj % NT
                            sync.wait_ge(sem["e3"], 2 * jj + 2)
                            sync.dma_start(out=s_par[tt2 * 128:(tt2 + 1) * 128, fb2 * 256:(fb2 + 1) * 256],
                                           in_=par_stage[jj % 2][:]).then_inc(sem["st3s%d" % (jj % 2)], 16)
                for jj in range(NJ - 2, NJ):
                    fb2, tt2 = jj // NT, jj % NT
                    sync.wait_ge(sem["e3"], 2 * jj + 2)
                    sync.dma_start(out=s_par[tt2 * 128:(tt2 + 1) * 128, fb2 * 256:(fb2 + 1) * 256],
                                   in_=par_stage[jj % 2][:]).then_inc(sem["st3s%d" % (jj % 2)], 16)
            L3_TOTAL = 5 * 16

            @block.gpsimd
            def _(g):
                wait_all(g, ST2_DONE + ST2V_DONE)
                for fb in range(NFB):
                    if fb >= 2:
                        g.wait_ge(sem["m3"], NT * (fb - 1))
                    g.dma_start(out=wf_blk[fb % 2][:],
                                in_=wf[fb].rearrange("p (kc m) -> p kc m", kc=NK)
                                ).then_inc(sem["w3s%d" % (fb % 2)], 16)

            @block.tensor
            def _(t):
                t.wait_ge(sem["l3"], L3_TOTAL)
                t.wait_ge(sem["e2v"], NVB * NT)
                for fb in range(NFB):
                    t.wait_ge(sem["w3s%d" % (fb % 2)], 16 * (fb // 2 + 1))
                    for tt in range(NT):
                        j = fb * NT + tt
                        if j >= 4:
                            t.wait_ge(sem["a3"], j - 3)
                        for k in range(NK):
                            ins = t.matmul(f_ps[j % 4][:, 0:256], xqT_sb[:, tt, k, :],
                                           wf_blk[fb % 2][:, k, :],
                                           start=(k == 0), stop=(k == NK - 1))
                        ins.then_inc(sem["m3"], 1)

            @block.scalar
            def _(s):
                s.wait_ge(sem["mos"], 16)
                for fb in range(NFB):
                    for tt in range(NT):
                        j = fb * NT + tt
                        s.wait_ge(sem["m3"], j + 1)
                        if j >= 2:
                            s.wait_ge(sem["e3"], 2 * (j - 2) + 1)
                        s.activation(y16[j % 2][:], f_ps[j % 4][:, 0:256], AF.Copy,
                                     scale=mos_sb[:, 0:1]).then_inc(sem["a3"], 1)

            @block.vector
            def _(v):
                for fb in range(NFB):
                    for tt in range(NT):
                        j = fb * NT + tt
                        v.wait_ge(sem["a3"], j + 1)
                        v.wait_ge(sem["x3s%d" % (j % 2)], 16 * (j // 2 + 1))
                        if j >= 2:
                            v.wait_ge(sem["st3s%d" % (j % 2)], 16 * (j // 2))
                        v.scalar_tensor_tensor(par_stage[j % 2][:], y16[j % 2][:],
                                               xs_all[:, tt:tt + 1], ws_sb[:, fb * 256:(fb + 1) * 256],
                                               ALU.mult, ALU.mult).then_inc(sem["e3"], 1)
                        v.tensor_tensor(par_stage[j % 2][:], par_stage[j % 2][:],
                                        x_strip[j % 2][:], ALU.add).then_inc(sem["e3"], 1)

        if phases < 4:
            drain(ST3_DONE)
            return nc

        m4_sc, m4_pv, a4_exp, d4_msk, d4_mrg = {}, {}, {}, {}, {}
        m4 = a4 = d4 = 0
        d4 += 2
        D4_SETUP = d4
        sc_list = []
        ei_of = {}
        for c in range(NCQ):
            ktl = 4 * (c + 1)
            for g in range(16):
                for hi in range(2):
                    for lt in range(ktl):
                        m4 += 1
                        m4_sc[(c, g, hi, lt)] = m4
                        sc_list.append((c, g, hi, lt))
                        a4 += 1
                        a4_exp[(c, g, hi, lt)] = a4
                        ei_of[(c, g, hi, lt)] = a4 - 1
                        if lt >= 4 * c:
                            d4 += 1
                            d4_msk[(c, g, hi, lt)] = d4
                        m4 += 1
                        m4_pv[(c, g, hi, lt)] = m4
                    d4 += 9
                    d4_mrg[(c, g, hi)] = d4

        with ExitStack() as p4s:
            pc4 = p4s.enter_context
            pt = [pc4(nc.sbuf_tensor(f"pt{i}", [128, D], F32)) for i in range(4)]
            kt_sb = [pc4(nc.sbuf_tensor(f"kt{i}", [128, T], BF16)) for i in range(2)]
            v_sb = [pc4(nc.sbuf_tensor(f"v4_{i}", [128, NT, 144], BF16)) for i in range(2)]
            qt_sb = [pc4(nc.sbuf_tensor(f"qt{i}", [128, 512], BF16)) for i in range(2)]
            e_sb = [pc4(nc.sbuf_tensor(f"e{i}", [128, 512], BF16)) for i in range(2)]
            sums4 = pc4(nc.sbuf_tensor("sums4", [128, 4], F32))
            rec4 = pc4(nc.sbuf_tensor("rec4", [128, 4], F32))
            sc_ps = [pc4(nc.psum_tensor(f"sc_ps{i}", [128, 512], F32)) for i in range(2)]
            pv_ps = [pc4(nc.psum_tensor(f"pv_ps{i}", [128, 512], F32)) for i in range(4)]

            @block.sync
            def _(sync):
                wait_all(sync, ST2_DONE + ST2V_DONE + ST3_DONE)
                for c in range(NCQ):
                    ktl = 4 * (c + 1)
                    kmax = 512 * (c + 1)
                    if c >= 1:
                        sync.wait_ge(sem["so"], 64 * c)
                    for q in range(4):
                        sync.dma_start(out=pt[q][:],
                                       in_=s_par[(4 * c + q) * 128:(4 * c + q + 1) * 128, :]
                                       ).then_inc(sem["pt4"], 16)
                    for g in range(16):
                        cg = c * 16 + g
                        if cg >= 2:
                            pc_, pg_ = (cg - 2) // 16, (cg - 2) % 16
                            sync.wait_ge(sem["m4"], m4_pv[(pc_, pg_, 1, 4 * (pc_ + 1) - 1)])
                        sync.dma_start(out=kt_sb[g % 2][:, 0:kmax],
                                       in_=s_kT[g * 128:(g + 1) * 128, 0:kmax]
                                       ).then_inc(sem["kt4s%d" % (g % 2)], 16)
                        sync.dma_start(out=v_sb[g % 2][:, 0:ktl, 0:128],
                                       in_=s_v[0:kmax, g * 128:(g + 1) * 128]
                                       .rearrange("(n p) d -> p n d", p=128)
                                       ).then_inc(sem["v4s%d" % (g % 2)], 16)
                        for hi, h in enumerate((g, g + 16)):
                            pi = cg * 2 + hi
                            if pi >= 2:
                                pcg, phi = (pi - 2) // 2, (pi - 2) % 2
                                pc_, pg_ = pcg // 16, pcg % 16
                                sync.wait_ge(sem["m4"], m4_sc[(pc_, pg_, phi, 4 * (pc_ + 1) - 1)])
                            sync.dma_start(out=qt_sb[pi % 2][:],
                                           in_=s_qT[h * 128:(h + 1) * 128, 512 * c:512 * (c + 1)]
                                           ).then_inc(sem["qt4s%d" % (pi % 2)], 16)
                    sync.wait_ge(sem["d4"], d4_mrg[(c, 15, 1)])
                    for q in range(4):
                        sync.dma_start(out=out[(4 * c + q) * 128:(4 * c + q + 1) * 128, :],
                                       in_=pt[q][:]).then_inc(sem["so"], 16)
                sync.wait_ge(sem["so"], 64 * NCQ)

            @block.tensor
            def _(t):
                ssi = 0
                for c in range(NCQ):
                    ktl = 4 * (c + 1)
                    for g in range(16):
                        cg = c * 16 + g
                        for hi in range(2):
                            pi = cg * 2 + hi
                            t.wait_ge(sem["kt4s%d" % (g % 2)], 16 * (cg // 2 + 1))
                            t.wait_ge(sem["v4s%d" % (g % 2)], 16 * (cg // 2 + 1))
                            t.wait_ge(sem["qt4s%d" % (pi % 2)], 16 * (pi // 2 + 1))
                            for lt in range(ktl):
                                if ssi >= 2:
                                    t.wait_ge(sem["a4"], ssi - 1)
                                t.matmul(sc_ps[ssi % 2][:], kt_sb[g % 2][:, lt * 128:(lt + 1) * 128],
                                         qt_sb[pi % 2][:],
                                         start=True, stop=True).then_inc(sem["m4"], 1)
                                ssi += 1
                                if lt >= 4 * c:
                                    t.wait_ge(sem["d4"], d4_msk[(c, g, hi, lt)])
                                else:
                                    t.wait_ge(sem["a4"], a4_exp[(c, g, hi, lt)])
                                if lt == 0:
                                    if pi >= 1:
                                        pcg, phi = (pi - 1) // 2, (pi - 1) % 2
                                        t.wait_ge(sem["d4"], d4_mrg[(pcg // 16, pcg % 16, phi)])
                                    else:
                                        t.wait_ge(sem["d4"], D4_SETUP)
                                for qv in range(max(0, lt - 4 * c), 4):
                                    ins = t.matmul(pv_ps[qv][:, 0:129],
                                                   e_sb[(a4_exp[(c, g, hi, lt)] - 1) % 2][:, qv * 128:(qv + 1) * 128],
                                                   v_sb[g % 2][:, lt, 0:129],
                                                   start=(lt == 0), stop=(lt == 4 * c + qv))
                                ins.then_inc(sem["m4"], 1)

            @block.scalar
            def _(s):
                for c in range(NCQ):
                    ktl = 4 * (c + 1)
                    for g in range(16):
                        for hi in range(2):
                            for lt in range(ktl):
                                ei = ei_of[(c, g, hi, lt)]
                                s.wait_ge(sem["m4"], m4_sc[(c, g, hi, lt)])
                                if ei >= 2:
                                    s.wait_ge(sem["m4"], m4_pv[sc_list[ei - 2]])
                                q0 = max(0, (lt - 4 * c) * 128)
                                s.activation(e_sb[ei % 2][:, q0:512], sc_ps[ei % 2][:, q0:512],
                                             AF.Exp, bias=cpool[:, 1:2],
                                             scale=cpool[:, 0:1]).then_inc(sem["a4"], 1)

            @block.vector
            def _(v):
                d4c = 2
                wait_all(v, ST3_DONE)
                v.memset(v_sb[0][:, :, 128:129], 1.0).then_inc(sem["d4"], 1)
                v.memset(v_sb[1][:, :, 128:129], 1.0).then_inc(sem["d4"], 1)
                for c in range(NCQ):
                    ktl = 4 * (c + 1)
                    for g in range(16):
                        for hi, h in enumerate((g, g + 16)):
                            for lt in range(4 * c, ktl):
                                ei = ei_of[(c, g, hi, lt)]
                                q0 = (lt - 4 * c) * 128
                                v.wait_ge(sem["a4"], a4_exp[(c, g, hi, lt)])
                                v.tensor_mul(e_sb[ei % 2][:, q0:q0 + 128],
                                             e_sb[ei % 2][:, q0:q0 + 128],
                                             mask128[:]).then_inc(sem["d4"], 1)
                                d4c += 1
                                assert d4c == d4_msk[(c, g, hi, lt)]
                            v.wait_ge(sem["m4"], m4_pv[(c, g, hi, ktl - 1)])
                            if g == 0 and hi == 0:
                                v.wait_ge(sem["pt4"], 64 * (c + 1))
                            for qv in range(4):
                                v.tensor_copy(sums4[:, qv:qv + 1], pv_ps[qv][:, 128:129]).then_inc(sem["d4"], 1)
                                d4c += 1
                            v.wait_ge(sem["d4"], d4c)
                            v.reciprocal(rec4[:], sums4[:]).then_inc(sem["d4"], 1)
                            d4c += 1
                            v.wait_ge(sem["d4"], d4c)
                            for qv in range(4):
                                v.scalar_tensor_tensor(pt[qv][:, h * 128:(h + 1) * 128],
                                                       pv_ps[qv][:, 0:128],
                                                       rec4[:, qv:qv + 1],
                                                       pt[qv][:, h * 128:(h + 1) * 128],
                                                       ALU.mult, ALU.add).then_inc(sem["d4"], 1)
                                d4c += 1
                            assert d4c == d4_mrg[(c, g, hi)]

    return nc


_NC_CACHE = {}


def kernel(L0_attn_x, L0_attn_Wq, L0_attn_bq, L0_attn_Wk, L0_attn_bk,
           L0_attn_Wv, L0_attn_bv, L0_ffn_w_int8, L0_ffn_weight_scale,
           L0_ffn_matmul_out_scale):
    x = np.asarray(L0_attn_x)
    f32 = np.float32

    def img_qk(W, ntile):
        W = np.asarray(W, dtype=f32)
        return np.ascontiguousarray(
            W.reshape(ntile, 128, NK, 128).transpose(0, 3, 2, 1).reshape(ntile, 128, NK * 128))

    def img_256(W, nb, dt):
        W = np.asarray(W)
        return np.ascontiguousarray(
            W.reshape(nb, 256, NK, 128).transpose(0, 3, 2, 1).reshape(nb, 128, NK * 256)).astype(dt, copy=False)

    wq_img = img_qk(L0_attn_Wq, 32)
    wk_img = img_qk(L0_attn_Wk, 16)
    wv_img = img_256(L0_attn_Wv, NVB, f32)
    wf_img = img_256(L0_ffn_w_int8, NFB, np.int8)
    bq_img = np.ascontiguousarray(np.asarray(L0_attn_bq, dtype=f32).reshape(32, 128).T)
    bk_img = np.ascontiguousarray(np.asarray(L0_attn_bk, dtype=f32).reshape(16, 128).T)
    bv_img = np.ascontiguousarray(np.broadcast_to(np.asarray(L0_attn_bv, dtype=f32)[None, :], (128, KVD)))
    ws_img = np.ascontiguousarray(np.broadcast_to(np.asarray(L0_ffn_weight_scale, dtype=f32)[None, :], (128, D)))
    mos_img = np.ascontiguousarray(np.broadcast_to(np.asarray(L0_ffn_matmul_out_scale, dtype=f32).reshape(1, 1), (128, 1)))

    if "nc" not in _NC_CACHE:
        _NC_CACHE["nc"] = build_nc()
    nc = _NC_CACHE["nc"]

    in_maps = []
    for b in range(8):
        in_maps.append({
            "x": np.ascontiguousarray(x[b], dtype=f32),
            "wq": wq_img, "wk": wk_img, "wv": wv_img, "wf": wf_img,
            "bq": bq_img, "bk": bk_img, "bv_rep": bv_img,
            "ws_rep": ws_img, "mos_rep": mos_img,
        })
    res = run_bass_kernel_spmd(nc, in_maps, core_ids=list(range(8)))
    return np.stack([np.asarray(r["out"]) for r in res.results], axis=0)


# version 07
# speedup vs baseline: 1.2978x; 1.2978x over previous
import numpy as np
import concourse.bass as bass
import concourse.mybir as mybir
from concourse.bass_utils import run_bass_kernel_spmd
from contextlib import ExitStack

F32 = mybir.dt.float32
BF16 = mybir.dt.bfloat16
FP16 = mybir.dt.float16
I8 = mybir.dt.int8
I32 = mybir.dt.int32
AF = mybir.ActivationFunctionType
ALU = mybir.AluOpType
AXX = mybir.AxisListType.X

T = 2048
D = 4096
KVD = 2048
NT = T // 128
NK = D // 128
N_QK = 48
NVB = KVD // 256
NFB = D // 256
NCQ = 4
SCALE = float(1.0 / np.sqrt(np.float32(32.0)))
SHIFT = 12.0
MAGIC = float(1.5 * 2.0 ** 23)
EPS = 1e-5


def build_nc(phases=4):
    nc = bass.Bass("TRN2", target_bir_lowering=False)

    x = nc.dram_tensor("x", [T, D], F32, kind="ExternalInput").ap()
    wq = nc.dram_tensor("wq", [32, 128, NK * 128], F32, kind="ExternalInput").ap()
    wk = nc.dram_tensor("wk", [16, 128, NK * 128], F32, kind="ExternalInput").ap()
    wv = nc.dram_tensor("wv", [NVB, 128, NK * 256], F32, kind="ExternalInput").ap()
    wf = nc.dram_tensor("wf", [NFB, 128, NK * 256], I8, kind="ExternalInput").ap()
    bq = nc.dram_tensor("bq", [128, 32], F32, kind="ExternalInput").ap()
    bk = nc.dram_tensor("bk", [128, 16], F32, kind="ExternalInput").ap()
    bv_rep = nc.dram_tensor("bv_rep", [128, KVD], F32, kind="ExternalInput").ap()
    ws_rep = nc.dram_tensor("ws_rep", [128, D], F32, kind="ExternalInput").ap()
    mos_rep = nc.dram_tensor("mos_rep", [128, 1], F32, kind="ExternalInput").ap()
    out = nc.dram_tensor("out", [T, D], F32, kind="ExternalOutput").ap()

    s_xnT = nc.dram_tensor("s_xnT", [128, NT * NK * 128], BF16, kind="Internal").ap()
    s_xqT = nc.dram_tensor("s_xqT", [128, NT * NK * 128], BF16, kind="Internal").ap()
    s_qT = nc.dram_tensor("s_qT", [D, T], BF16, kind="Internal").ap()
    s_kT = nc.dram_tensor("s_kT", [KVD, T], BF16, kind="Internal").ap()
    s_v = nc.dram_tensor("s_v", [T, KVD], BF16, kind="Internal").ap()
    s_par = nc.dram_tensor("s_par", [T, D], F32, kind="Internal").ap()
    xnT3 = s_xnT.rearrange("p (tt kc m) -> p tt kc m", tt=NT, kc=NK)
    xqT3 = s_xqT.rearrange("p (tt kc m) -> p tt kc m", tt=NT, kc=NK)

    with ExitStack() as es:
        ec = es.enter_context
        cpool = ec(nc.sbuf_tensor("cpool", [128, 4], F32))
        mos_sb = ec(nc.sbuf_tensor("mos_sb", [128, 1], F32))
        xs_all = ec(nc.sbuf_tensor("xs_all", [128, NT], F32))
        iota32 = ec(nc.sbuf_tensor("iota32", [128, 128], I32))
        mask128 = ec(nc.sbuf_tensor("mask128", [128, 128], BF16))
        masku = ec(nc.sbuf_tensor("masku", [128, 128], BF16))
        ident = ec(nc.sbuf_tensor("ident", [128, 128], BF16))

        sem = {}
        for name in ["mos",
                     "x1s0", "x1s1",
                     "a1", "d1", "p1", "e1",
                     "st1s0", "st1s1", "st1s2", "st1s3",
                     "w2s0", "w2s1", "l2", "m2", "e2",
                     "st2s0", "st2s1",
                     "w2vs0", "w2vs1", "m2v", "e2v",
                     "st2vs0", "st2vs1", "st2vs2", "st2vs3",
                     "w3s0", "w3s1", "l3",
                     "x3s0", "x3s1", "m3", "a3", "e3",
                     "st3s0", "st3s1",
                     "pt4", "kt4s0", "kt4s1", "v4s0", "v4s1", "qt4s0", "qt4s1",
                     "m4", "a4", "d4", "so"]:
            sem[name] = ec(nc.semaphore("s_" + name))

        block = ec(nc.Block())

        ST1_DONE = [("st1s%d" % j, 16 * 8) for j in range(4)]
        ST2_DONE = [("st2s%d" % j, 16 * 24) for j in range(2)]
        ST2V_DONE = [("st2vs%d" % j, 16 * 32) for j in range(4)]
        ST3_DONE = [("st3s%d" % j, 16 * 128) for j in range(2)]

        def wait_all(eng, pairs):
            for name, val in pairs:
                eng.wait_ge(sem[name], val)

        def drain(pairs):
            @block.sync
            def _(sync):
                wait_all(sync, pairs)

        @block.gpsimd
        def _(g):
            g.iota(iota32[:], pattern=[[-1, 128]], base=0, channel_multiplier=1).then_inc(sem["d1"], 1)

        @block.sync
        def _(sync):
            sync.dma_start(out=mos_sb[:], in_=mos_rep[:]).then_inc(sem["mos"], 16)

        @block.vector
        def _(v):
            v.wait_ge(sem["d1"], 1)
            v.memset(cpool[:, 0:1], SCALE)
            v.memset(cpool[:, 1:2], -SHIFT)
            v.memset(cpool[:, 2:3], EPS)
            v.tensor_single_scalar(mask128[:], iota32[:], 0.0, ALU.is_le)
            v.tensor_single_scalar(masku[:], iota32[:], 0.0, ALU.is_lt)
            v.wait_ge(sem["d1"], 1)
            v.tensor_scalar(masku[:], masku[:], -1.0e9, None, ALU.mult)
            v.tensor_single_scalar(ident[:], iota32[:], 0.0, ALU.is_equal).then_inc(sem["d1"], 1)

        SET_D1 = 2
        DP = 16

        def d1(i, k):
            return SET_D1 + i * DP + k

        with ExitStack() as p1s:
            pc1 = p1s.enter_context
            x_t = [pc1(nc.sbuf_tensor(f"x_t{i}", [128, D], F32)) for i in range(2)]
            xn_bf = [pc1(nc.sbuf_tensor(f"xn_bf{i}", [128, D], BF16)) for i in range(2)]
            xq_bf = [pc1(nc.sbuf_tensor(f"xq_bf{i}", [128, D], BF16)) for i in range(2)]
            qf = pc1(nc.sbuf_tensor("qf", [128, D], F32))
            qi8 = [pc1(nc.sbuf_tensor(f"qi8_{i}", [128, D], I8)) for i in range(2)]
            jnk_bf = pc1(nc.sbuf_tensor("jnk_bf", [128, D], BF16))
            stat = pc1(nc.sbuf_tensor("stat", [128, 2, 10], F32))
            xs16 = pc1(nc.sbuf_tensor("xs16", [128, 2], FP16))
            strip = [pc1(nc.sbuf_tensor(f"strip{i}", [128, NK, 128], BF16)) for i in range(4)]
            tp_ps = [pc1(nc.psum_tensor(f"tp_ps{i}", [128, 1024], BF16)) for i in range(2)]

            def emit_strip_store(sync, i):
                for half in range(2):
                    slot = 2 * (i % 2) + half
                    sync.wait_ge(sem["e1"], 8 * i + 4 * half + 4)
                    dst = xnT3 if half == 0 else xqT3
                    sync.dma_start(out=dst[:, i, :, :],
                                   in_=strip[slot][:]).then_inc(sem["st1s%d" % slot], 16)

            @block.sync
            def _(sync):
                for i in range(NT):
                    if i >= 2:
                        sync.wait_ge(sem["d1"], d1(i - 2, 13))
                        sync.wait_ge(sem["a1"], 4 * (i - 2) + 2)
                    sync.dma_start(out=x_t[i % 2][:],
                                   in_=x[i * 128:(i + 1) * 128, :]).then_inc(sem["x1s%d" % (i % 2)], 16)
                    if i >= 2:
                        emit_strip_store(sync, i - 2)
                for i in range(NT - 2, NT):
                    emit_strip_store(sync, i)

            @block.scalar
            def _(s):
                for i in range(NT):
                    p = i % 2
                    s.wait_ge(sem["x1s%d" % p], 16 * (i // 2 + 1))
                    s.activation(jnk_bf[:], x_t[p][:], AF.Copy, accum_out=stat[:, p, 0:1])
                    s.activation(jnk_bf[:], x_t[p][:], AF.Square, accum_out=stat[:, p, 1:2]).then_inc(sem["a1"], 2)
                    s.wait_ge(sem["d1"], d1(i, 4))
                    s.activation(stat[:, p, 5:6], stat[:, p, 4:5], AF.Sqrt, bias=cpool[:, 2:3]).then_inc(sem["a1"], 1)
                    s.wait_ge(sem["d1"], d1(i, 16))
                    if i >= 2:
                        s.wait_ge(sem["p1"], 8 * (i - 2) + 8)
                    s.copy(xq_bf[p][:], qi8[p][:]).then_inc(sem["a1"], 1)

            @block.vector
            def _(v):
                dcnt = [SET_D1]

                def op(ins):
                    ins.then_inc(sem["d1"], 1)
                    dcnt[0] += 1

                def sw():
                    v.wait_ge(sem["d1"], dcnt[0])

                for i in range(NT):
                    p = i % 2
                    c = stat[:, p, :]
                    assert dcnt[0] == d1(i, 0)
                    v.wait_ge(sem["a1"], 4 * i + 2)
                    op(v.tensor_scalar(c[:, 2:3], c[:, 0:1], 1.0 / D, None, ALU.mult))
                    op(v.tensor_scalar(c[:, 3:4], c[:, 1:2], 1.0 / D, None, ALU.mult))
                    sw()
                    op(v.tensor_tensor(c[:, 4:5], c[:, 2:3], c[:, 2:3], ALU.mult))
                    sw()
                    op(v.tensor_tensor(c[:, 4:5], c[:, 3:4], c[:, 4:5], ALU.subtract))
                    v.wait_ge(sem["a1"], 4 * i + 3)
                    op(v.reciprocal(c[:, 6:7], c[:, 5:6]))
                    if i >= 2:
                        v.wait_ge(sem["p1"], 8 * (i - 2) + 4)
                    sw()
                    op(v.tensor_scalar(xn_bf[p][:], x_t[p][:], c[:, 2:3], c[:, 6:7],
                                       ALU.subtract, ALU.mult))
                    op(v.reduce_max(c[:, 7:8], xn_bf[p][:], AXX, apply_absolute_value=True))
                    sw()
                    op(v.tensor_scalar(c[:, 7:8], c[:, 7:8], 1.0 / 127.0, None, ALU.mult))
                    sw()
                    op(v.tensor_copy(xs16[:, p:p + 1], c[:, 7:8]))
                    sw()
                    op(v.tensor_copy(xs_all[:, i:i + 1], xs16[:, p:p + 1]))
                    op(v.reciprocal(c[:, 8:9], c[:, 7:8]))
                    sw()
                    op(v.tensor_tensor(c[:, 9:10], c[:, 6:7], c[:, 8:9], ALU.mult))
                    sw()
                    op(v.tensor_scalar(qf[:], x_t[p][:], c[:, 2:3], c[:, 9:10],
                                       ALU.subtract, ALU.mult))
                    op(v.tensor_scalar(qf[:], qf[:], MAGIC, None, ALU.add))
                    op(v.tensor_scalar(qf[:], qf[:], -MAGIC, None, ALU.add))
                    op(v.tensor_copy(qi8[p][:], qf[:]))
                    assert dcnt[0] == d1(i, DP)
                    for half in range(2):
                        slot = 2 * p + half
                        for bg in range(4):
                            g_ = 8 * i + 4 * half + bg
                            v.wait_ge(sem["p1"], g_ + 1)
                            if i >= 2 and bg == 0:
                                v.wait_ge(sem["st1s%d" % slot], 16 * (i // 2))
                            v.tensor_copy(strip[slot][:, 8 * bg:8 * bg + 8, :],
                                          tp_ps[g_ % 2][:]).then_inc(sem["e1"], 1)

            @block.tensor
            def _(t):
                for i in range(NT):
                    p = i % 2
                    for half in range(2):
                        src = xn_bf[p] if half == 0 else xq_bf[p]
                        if half == 0:
                            t.wait_ge(sem["d1"], d1(i, 6))
                        else:
                            t.wait_ge(sem["a1"], 4 * i + 4)
                        for bg in range(4):
                            g_ = 8 * i + 4 * half + bg
                            if g_ >= 2:
                                t.wait_ge(sem["e1"], g_ - 1)
                            for j in range(8):
                                col = 8 * bg + j
                                ins = t.matmul(tp_ps[g_ % 2][:, j * 128:(j + 1) * 128],
                                               src[:, col * 128:(col + 1) * 128], ident[:],
                                               is_transpose=True, start=(j == 0), stop=(j == 7))
                            ins.then_inc(sem["p1"], 1)

        if phases < 2:
            drain(ST1_DONE)
            return nc

        with ExitStack() as p2s:
            pc2 = p2s.enter_context
            xnT_sb = pc2(nc.sbuf_tensor("xnT_sb", [128, NT, NK, 128], BF16))
            bq_sb = pc2(nc.sbuf_tensor("bq_sb", [128, 32], F32))
            bk_sb = pc2(nc.sbuf_tensor("bk_sb", [128, 16], F32))
            bv_sb = pc2(nc.sbuf_tensor("bv_sb", [128, KVD], F32))
            wqk_blk = [pc2(nc.sbuf_tensor(f"wqk{i}", [128, NK, 128], BF16)) for i in range(2)]
            qk_stage = [pc2(nc.sbuf_tensor(f"qks{i}", [128, T], BF16)) for i in range(2)]
            wv_blk = [pc2(nc.sbuf_tensor(f"wvb{i}", [128, NK, 256], BF16)) for i in range(2)]
            v_stage = [pc2(nc.sbuf_tensor(f"vst{i}", [128, 256], BF16)) for i in range(4)]

            @block.sync
            def _(sync):
                wait_all(sync, ST1_DONE)
                for q4 in range(4):
                    sync.dma_start(out=xnT_sb[:, q4 * 4:(q4 + 1) * 4, :, :],
                                   in_=xnT3[:, q4 * 4:(q4 + 1) * 4, :, :]).then_inc(sem["l2"], 16)
                sync.dma_start(out=bq_sb[:], in_=bq[:]).then_inc(sem["l2"], 16)
                sync.dma_start(out=bk_sb[:], in_=bk[:]).then_inc(sem["l2"], 16)
                sync.dma_start(out=bv_sb[:], in_=bv_rep[:]).then_inc(sem["l2"], 16)
            L2_TOTAL = 7 * 16

            @block.gpsimd
            def _(g):
                wait_all(g, ST1_DONE)
                for nt in range(N_QK):
                    if nt >= 2:
                        g.wait_ge(sem["m2"], 4 * (nt - 2) + 4)
                    src = wq[nt] if nt < 32 else wk[nt - 32]
                    g.dma_start(out=wqk_blk[nt % 2][:],
                                in_=src.rearrange("p (kc m) -> p kc m", kc=NK)
                                ).then_inc(sem["w2s%d" % (nt % 2)], 16)
                for vb in range(NVB):
                    if vb >= 2:
                        g.wait_ge(sem["m2v"], 16 * (vb - 1))
                    g.dma_start(out=wv_blk[vb % 2][:],
                                in_=wv[vb].rearrange("p (kc m) -> p kc m", kc=NK)
                                ).then_inc(sem["w2vs%d" % (vb % 2)], 16)

            with ExitStack() as p2qk:
                qk_ps = [p2qk.enter_context(nc.psum_tensor(f"qk_ps{i}", [128, T], F32)) for i in range(2)]

                @block.tensor
                def _(t):
                    t.wait_ge(sem["l2"], L2_TOTAL)
                    for nt in range(N_QK):
                        t.wait_ge(sem["w2s%d" % (nt % 2)], 16 * (nt // 2 + 1))
                        for tc in range(4):
                            if nt >= 2:
                                t.wait_ge(sem["e2"], 4 * (nt - 2) + tc + 1)
                            for k in range(NK):
                                ins = t.matmul(qk_ps[nt % 2][:, tc * 512:(tc + 1) * 512],
                                               wqk_blk[nt % 2][:, k, :],
                                               xnT_sb[:, 4 * tc:4 * (tc + 1), k, :],
                                               start=(k == 0), stop=(k == NK - 1))
                            ins.then_inc(sem["m2"], 1)

                @block.vector
                def _(v):
                    for nt in range(N_QK):
                        for tc in range(4):
                            v.wait_ge(sem["m2"], 4 * nt + tc + 1)
                            if nt >= 2 and tc == 0:
                                v.wait_ge(sem["st2s%d" % (nt % 2)], 16 * (nt // 2))
                            bias = bq_sb[:, nt:nt + 1] if nt < 32 else bk_sb[:, nt - 32:nt - 31]
                            v.tensor_scalar(qk_stage[nt % 2][:, tc * 512:(tc + 1) * 512],
                                            qk_ps[nt % 2][:, tc * 512:(tc + 1) * 512],
                                            bias, None, ALU.add).then_inc(sem["e2"], 1)

                @block.sync
                def _(sync):
                    for nt in range(N_QK):
                        sync.wait_ge(sem["e2"], 4 * nt + 4)
                        dst = s_qT[nt * 128:(nt + 1) * 128, :] if nt < 32 else \
                            s_kT[(nt - 32) * 128:(nt - 31) * 128, :]
                        sync.dma_start(out=dst, in_=qk_stage[nt % 2][:]
                                       ).then_inc(sem["st2s%d" % (nt % 2)], 16)

            v_ps = [pc2(nc.psum_tensor(f"v_ps{i}", [128, 512], F32)) for i in range(4)]

            @block.tensor
            def _(t):
                t.wait_ge(sem["e2"], 4 * N_QK)
                for vb in range(NVB):
                    t.wait_ge(sem["w2vs%d" % (vb % 2)], 16 * (vb // 2 + 1))
                    for tt in range(NT):
                        j = vb * NT + tt
                        if j >= 4:
                            t.wait_ge(sem["e2v"], j - 3)
                        for k in range(NK):
                            ins = t.matmul(v_ps[j % 4][:, 0:256], xnT_sb[:, tt, k, :],
                                           wv_blk[vb % 2][:, k, :],
                                           start=(k == 0), stop=(k == NK - 1))
                        ins.then_inc(sem["m2v"], 1)

            @block.vector
            def _(v):
                for vb in range(NVB):
                    for tt in range(NT):
                        j = vb * NT + tt
                        v.wait_ge(sem["m2v"], j + 1)
                        if j >= 4:
                            v.wait_ge(sem["st2vs%d" % (j % 4)], 16 * (j // 4))
                        v.tensor_tensor(v_stage[j % 4][:], v_ps[j % 4][:, 0:256],
                                        bv_sb[:, vb * 256:(vb + 1) * 256], ALU.add).then_inc(sem["e2v"], 1)

            @block.sync
            def _(sync):
                for vb in range(NVB):
                    for tt in range(NT):
                        j = vb * NT + tt
                        sync.wait_ge(sem["e2v"], j + 1)
                        sync.dma_start(out=s_v[tt * 128:(tt + 1) * 128, vb * 256:(vb + 1) * 256],
                                       in_=v_stage[j % 4][:]).then_inc(sem["st2vs%d" % (j % 4)], 16)

        if phases < 3:
            drain(ST2_DONE + ST2V_DONE)
            return nc

        with ExitStack() as p3s:
            pc3 = p3s.enter_context
            xqT_sb = pc3(nc.sbuf_tensor("xqT_sb", [128, NT, NK, 128], BF16))
            ws_sb = pc3(nc.sbuf_tensor("ws_sb", [128, D], F32))
            wf_blk = [pc3(nc.sbuf_tensor(f"wfb{i}", [128, NK, 256], BF16)) for i in range(2)]
            y16 = [pc3(nc.sbuf_tensor(f"y16_{i}", [128, 256], FP16)) for i in range(2)]
            x_strip = [pc3(nc.sbuf_tensor(f"xs3_{i}", [128, 256], F32)) for i in range(2)]
            par_stage = [pc3(nc.sbuf_tensor(f"par{i}", [128, 256], F32)) for i in range(2)]
            f_ps = [pc3(nc.psum_tensor(f"f_ps{i}", [128, 512], F32)) for i in range(4)]

            NJ = NFB * NT

            @block.sync
            def _(sync):
                wait_all(sync, ST1_DONE + ST2_DONE + ST2V_DONE)
                for q4 in range(4):
                    sync.dma_start(out=xqT_sb[:, q4 * 4:(q4 + 1) * 4, :, :],
                                   in_=xqT3[:, q4 * 4:(q4 + 1) * 4, :, :]).then_inc(sem["l3"], 16)
                sync.dma_start(out=ws_sb[:], in_=ws_rep[:]).then_inc(sem["l3"], 16)
                for fb in range(NFB):
                    for tt in range(NT):
                        j = fb * NT + tt
                        if j >= 2:
                            sync.wait_ge(sem["e3"], 2 * (j - 2) + 2)
                        sync.dma_start(out=x_strip[j % 2][:],
                                       in_=x[tt * 128:(tt + 1) * 128, fb * 256:(fb + 1) * 256]
                                       ).then_inc(sem["x3s%d" % (j % 2)], 16)
                        if j >= 2:
                            jj = j - 2
                            fb2, tt2 = jj // NT, jj % NT
                            sync.wait_ge(sem["e3"], 2 * jj + 2)
                            sync.dma_start(out=s_par[tt2 * 128:(tt2 + 1) * 128, fb2 * 256:(fb2 + 1) * 256],
                                           in_=par_stage[jj % 2][:]).then_inc(sem["st3s%d" % (jj % 2)], 16)
                for jj in range(NJ - 2, NJ):
                    fb2, tt2 = jj // NT, jj % NT
                    sync.wait_ge(sem["e3"], 2 * jj + 2)
                    sync.dma_start(out=s_par[tt2 * 128:(tt2 + 1) * 128, fb2 * 256:(fb2 + 1) * 256],
                                   in_=par_stage[jj % 2][:]).then_inc(sem["st3s%d" % (jj % 2)], 16)
            L3_TOTAL = 5 * 16

            @block.gpsimd
            def _(g):
                wait_all(g, ST2_DONE + ST2V_DONE)
                for fb in range(NFB):
                    if fb >= 2:
                        g.wait_ge(sem["m3"], NT * (fb - 1))
                    g.dma_start(out=wf_blk[fb % 2][:],
                                in_=wf[fb].rearrange("p (kc m) -> p kc m", kc=NK)
                                ).then_inc(sem["w3s%d" % (fb % 2)], 16)

            @block.tensor
            def _(t):
                t.wait_ge(sem["l3"], L3_TOTAL)
                t.wait_ge(sem["e2v"], NVB * NT)
                for fb in range(NFB):
                    t.wait_ge(sem["w3s%d" % (fb % 2)], 16 * (fb // 2 + 1))
                    for tt in range(NT):
                        j = fb * NT + tt
                        if j >= 4:
                            t.wait_ge(sem["a3"], j - 3)
                        for k in range(NK):
                            ins = t.matmul(f_ps[j % 4][:, 0:256], xqT_sb[:, tt, k, :],
                                           wf_blk[fb % 2][:, k, :],
                                           start=(k == 0), stop=(k == NK - 1))
                        ins.then_inc(sem["m3"], 1)

            @block.scalar
            def _(s):
                s.wait_ge(sem["mos"], 16)
                for fb in range(NFB):
                    for tt in range(NT):
                        j = fb * NT + tt
                        s.wait_ge(sem["m3"], j + 1)
                        if j >= 2:
                            s.wait_ge(sem["e3"], 2 * (j - 2) + 1)
                        s.activation(y16[j % 2][:], f_ps[j % 4][:, 0:256], AF.Copy,
                                     scale=mos_sb[:, 0:1]).then_inc(sem["a3"], 1)

            @block.vector
            def _(v):
                for fb in range(NFB):
                    for tt in range(NT):
                        j = fb * NT + tt
                        v.wait_ge(sem["a3"], j + 1)
                        v.wait_ge(sem["x3s%d" % (j % 2)], 16 * (j // 2 + 1))
                        if j >= 2:
                            v.wait_ge(sem["st3s%d" % (j % 2)], 16 * (j // 2))
                        v.scalar_tensor_tensor(par_stage[j % 2][:], y16[j % 2][:],
                                               xs_all[:, tt:tt + 1], ws_sb[:, fb * 256:(fb + 1) * 256],
                                               ALU.mult, ALU.mult).then_inc(sem["e3"], 1)
                        v.tensor_tensor(par_stage[j % 2][:], par_stage[j % 2][:],
                                        x_strip[j % 2][:], ALU.add).then_inc(sem["e3"], 1)

        if phases < 4:
            drain(ST3_DONE)
            return nc

        m4_sc, m4_pv, a4_exp, d4_mrg = {}, {}, {}, {}
        a4 = d4 = 0
        d4 += 2
        D4_SETUP = d4
        sc_list = []
        ei_of = {}
        work = []
        for c in range(NCQ):
            ktl = 4 * (c + 1)
            for g in range(16):
                for hi in range(2):
                    for lt in range(ktl):
                        sc_list.append((c, g, hi, lt))
                        work.append((c, g, hi, lt))
                        a4 += 1
                        a4_exp[(c, g, hi, lt)] = a4
                        ei_of[(c, g, hi, lt)] = a4 - 1
                    d4 += 9
                    d4_mrg[(c, g, hi)] = d4
        pe_order = []
        emitted_sc = set()
        for i in range(len(work)):
            if i == 0:
                pe_order.append(("sc", 0))
                emitted_sc.add(0)
            if i + 1 < len(work) and work[i + 1][0] == work[i][0]:
                pe_order.append(("sc", i + 1))
                emitted_sc.add(i + 1)
            pe_order.append(("pv", i))
            if i + 1 < len(work) and (i + 1) not in emitted_sc:
                pe_order.append(("sc", i + 1))
                emitted_sc.add(i + 1)
        m4 = 0
        for kind, i in pe_order:
            m4 += 1
            if kind == "sc":
                m4_sc[work[i]] = m4
            else:
                m4_pv[work[i]] = m4

        with ExitStack() as p4s:
            pc4 = p4s.enter_context
            pt = [pc4(nc.sbuf_tensor(f"pt{i}", [128, D], F32)) for i in range(4)]
            kt_sb = [pc4(nc.sbuf_tensor(f"kt{i}", [128, T], BF16)) for i in range(2)]
            v_sb = [pc4(nc.sbuf_tensor(f"v4_{i}", [128, NT, 144], BF16)) for i in range(2)]
            qt_sb = [pc4(nc.sbuf_tensor(f"qt{i}", [128, 512], BF16)) for i in range(2)]
            e_sb = [pc4(nc.sbuf_tensor(f"e{i}", [128, 512], BF16)) for i in range(2)]
            sums4 = pc4(nc.sbuf_tensor("sums4", [128, 4], F32))
            rec4 = pc4(nc.sbuf_tensor("rec4", [128, 4], F32))
            sc_ps = [pc4(nc.psum_tensor(f"sc_ps{i}", [128, 512], F32)) for i in range(2)]
            pv_ps = [pc4(nc.psum_tensor(f"pv_ps{i}", [128, 512], F32)) for i in range(4)]

            @block.sync
            def _(sync):
                wait_all(sync, ST2_DONE + ST2V_DONE + ST3_DONE)
                for c in range(NCQ):
                    ktl = 4 * (c + 1)
                    kmax = 512 * (c + 1)
                    if c >= 1:
                        sync.wait_ge(sem["so"], 64 * c)
                    for q in range(4):
                        sync.dma_start(out=pt[q][:],
                                       in_=s_par[(4 * c + q) * 128:(4 * c + q + 1) * 128, :]
                                       ).then_inc(sem["pt4"], 16)
                    for g in range(16):
                        cg = c * 16 + g
                        if cg >= 2:
                            pc_, pg_ = (cg - 2) // 16, (cg - 2) % 16
                            sync.wait_ge(sem["m4"], m4_pv[(pc_, pg_, 1, 4 * (pc_ + 1) - 1)])
                        sync.dma_start(out=kt_sb[g % 2][:, 0:kmax],
                                       in_=s_kT[g * 128:(g + 1) * 128, 0:kmax]
                                       ).then_inc(sem["kt4s%d" % (g % 2)], 16)
                        sync.dma_start(out=v_sb[g % 2][:, 0:ktl, 0:128],
                                       in_=s_v[0:kmax, g * 128:(g + 1) * 128]
                                       .rearrange("(n p) d -> p n d", p=128)
                                       ).then_inc(sem["v4s%d" % (g % 2)], 16)
                        for hi, h in enumerate((g, g + 16)):
                            pi = cg * 2 + hi
                            if pi >= 2:
                                pcg, phi = (pi - 2) // 2, (pi - 2) % 2
                                pc_, pg_ = pcg // 16, pcg % 16
                                sync.wait_ge(sem["m4"], m4_sc[(pc_, pg_, phi, 4 * (pc_ + 1) - 1)])
                            sync.dma_start(out=qt_sb[pi % 2][:],
                                           in_=s_qT[h * 128:(h + 1) * 128, 512 * c:512 * (c + 1)]
                                           ).then_inc(sem["qt4s%d" % (pi % 2)], 16)
                    sync.wait_ge(sem["d4"], d4_mrg[(c, 15, 1)])
                    for q in range(4):
                        sync.dma_start(out=out[(4 * c + q) * 128:(4 * c + q + 1) * 128, :],
                                       in_=pt[q][:]).then_inc(sem["so"], 16)
                sync.wait_ge(sem["so"], 64 * NCQ)

            @block.tensor
            def _(t):
                def emit_scores2(i):
                    c, g, hi, lt = work[i]
                    cg = c * 16 + g
                    pi = cg * 2 + hi
                    if lt == 0:
                        t.wait_ge(sem["kt4s%d" % (g % 2)], 16 * (cg // 2 + 1))
                        t.wait_ge(sem["v4s%d" % (g % 2)], 16 * (cg // 2 + 1))
                        t.wait_ge(sem["qt4s%d" % (pi % 2)], 16 * (pi // 2 + 1))
                    if i >= 2:
                        t.wait_ge(sem["a4"], i - 1)
                    diag = lt >= 4 * c
                    ins = t.matmul(sc_ps[i % 2][:], kt_sb[g % 2][:, lt * 128:(lt + 1) * 128],
                                   qt_sb[pi % 2][:], start=True, stop=not diag)
                    if diag:
                        q0 = (lt - 4 * c) * 128
                        ins = t.matmul(sc_ps[i % 2][:, q0:q0 + 128], masku[:], ident[:],
                                       start=False, stop=True)
                    ins.then_inc(sem["m4"], 1)

                for kind, i in pe_order:
                    c, g, hi, lt = work[i]
                    if kind == "sc":
                        emit_scores2(i)
                        continue
                    t.wait_ge(sem["a4"], a4_exp[(c, g, hi, lt)])
                    if lt == 0:
                        pi = (c * 16 + g) * 2 + hi
                        if pi >= 1:
                            pcg, phi = (pi - 1) // 2, (pi - 1) % 2
                            t.wait_ge(sem["d4"], d4_mrg[(pcg // 16, pcg % 16, phi)])
                        else:
                            t.wait_ge(sem["d4"], D4_SETUP)
                    for qv in range(max(0, lt - 4 * c), 4):
                        ins = t.matmul(pv_ps[qv][:, 0:129],
                                       e_sb[(a4_exp[(c, g, hi, lt)] - 1) % 2][:, qv * 128:(qv + 1) * 128],
                                       v_sb[g % 2][:, lt, 0:129],
                                       start=(lt == 0), stop=(lt == 4 * c + qv))
                    ins.then_inc(sem["m4"], 1)

            @block.scalar
            def _(s):
                for c in range(NCQ):
                    ktl = 4 * (c + 1)
                    for g in range(16):
                        for hi in range(2):
                            for lt in range(ktl):
                                ei = ei_of[(c, g, hi, lt)]
                                s.wait_ge(sem["m4"], m4_sc[(c, g, hi, lt)])
                                if ei >= 2:
                                    s.wait_ge(sem["m4"], m4_pv[sc_list[ei - 2]])
                                q0 = max(0, (lt - 4 * c) * 128)
                                s.activation(e_sb[ei % 2][:, q0:512], sc_ps[ei % 2][:, q0:512],
                                             AF.Exp, bias=cpool[:, 1:2],
                                             scale=cpool[:, 0:1]).then_inc(sem["a4"], 1)

            @block.vector
            def _(v):
                d4c = 2
                wait_all(v, ST3_DONE)
                v.memset(v_sb[0][:, :, 128:129], 1.0).then_inc(sem["d4"], 1)
                v.memset(v_sb[1][:, :, 128:129], 1.0).then_inc(sem["d4"], 1)
                for c in range(NCQ):
                    ktl = 4 * (c + 1)
                    for g in range(16):
                        for hi, h in enumerate((g, g + 16)):
                            v.wait_ge(sem["m4"], m4_pv[(c, g, hi, ktl - 1)])
                            if g == 0 and hi == 0:
                                v.wait_ge(sem["pt4"], 64 * (c + 1))
                            for qv in range(4):
                                v.tensor_copy(sums4[:, qv:qv + 1], pv_ps[qv][:, 128:129]).then_inc(sem["d4"], 1)
                                d4c += 1
                            v.wait_ge(sem["d4"], d4c)
                            v.reciprocal(rec4[:], sums4[:]).then_inc(sem["d4"], 1)
                            d4c += 1
                            v.wait_ge(sem["d4"], d4c)
                            for qv in range(4):
                                v.scalar_tensor_tensor(pt[qv][:, h * 128:(h + 1) * 128],
                                                       pv_ps[qv][:, 0:128],
                                                       rec4[:, qv:qv + 1],
                                                       pt[qv][:, h * 128:(h + 1) * 128],
                                                       ALU.mult, ALU.add).then_inc(sem["d4"], 1)
                                d4c += 1
                            assert d4c == d4_mrg[(c, g, hi)]

    return nc


_NC_CACHE = {}


def kernel(L0_attn_x, L0_attn_Wq, L0_attn_bq, L0_attn_Wk, L0_attn_bk,
           L0_attn_Wv, L0_attn_bv, L0_ffn_w_int8, L0_ffn_weight_scale,
           L0_ffn_matmul_out_scale):
    x = np.asarray(L0_attn_x)
    f32 = np.float32

    def img_qk(W, ntile):
        W = np.asarray(W, dtype=f32)
        return np.ascontiguousarray(
            W.reshape(ntile, 128, NK, 128).transpose(0, 3, 2, 1).reshape(ntile, 128, NK * 128))

    def img_256(W, nb, dt):
        W = np.asarray(W)
        return np.ascontiguousarray(
            W.reshape(nb, 256, NK, 128).transpose(0, 3, 2, 1).reshape(nb, 128, NK * 256)).astype(dt, copy=False)

    wq_img = img_qk(L0_attn_Wq, 32)
    wk_img = img_qk(L0_attn_Wk, 16)
    wv_img = img_256(L0_attn_Wv, NVB, f32)
    wf_img = img_256(L0_ffn_w_int8, NFB, np.int8)
    bq_img = np.ascontiguousarray(np.asarray(L0_attn_bq, dtype=f32).reshape(32, 128).T)
    bk_img = np.ascontiguousarray(np.asarray(L0_attn_bk, dtype=f32).reshape(16, 128).T)
    bv_img = np.ascontiguousarray(np.broadcast_to(np.asarray(L0_attn_bv, dtype=f32)[None, :], (128, KVD)))
    ws_img = np.ascontiguousarray(np.broadcast_to(np.asarray(L0_ffn_weight_scale, dtype=f32)[None, :], (128, D)))
    mos_img = np.ascontiguousarray(np.broadcast_to(np.asarray(L0_ffn_matmul_out_scale, dtype=f32).reshape(1, 1), (128, 1)))

    if "nc" not in _NC_CACHE:
        _NC_CACHE["nc"] = build_nc()
    nc = _NC_CACHE["nc"]

    in_maps = []
    for b in range(8):
        in_maps.append({
            "x": np.ascontiguousarray(x[b], dtype=f32),
            "wq": wq_img, "wk": wk_img, "wv": wv_img, "wf": wf_img,
            "bq": bq_img, "bk": bk_img, "bv_rep": bv_img,
            "ws_rep": ws_img, "mos_rep": mos_img,
        })
    res = run_bass_kernel_spmd(nc, in_maps, core_ids=list(range(8)))
    return np.stack([np.asarray(r["out"]) for r in res.results], axis=0)


# version 08
# speedup vs baseline: 1.3101x; 1.0095x over previous
import numpy as np
import concourse.bass as bass
import concourse.mybir as mybir
from concourse.bass_utils import run_bass_kernel_spmd
from contextlib import ExitStack

F32 = mybir.dt.float32
BF16 = mybir.dt.bfloat16
FP16 = mybir.dt.float16
I8 = mybir.dt.int8
I32 = mybir.dt.int32
AF = mybir.ActivationFunctionType
ALU = mybir.AluOpType
AXX = mybir.AxisListType.X

T = 2048
D = 4096
KVD = 2048
NT = T // 128
NK = D // 128
N_QK = 48
NVB = KVD // 256
NFB = D // 256
NCQ = 4
SCALE = float(1.0 / np.sqrt(np.float32(32.0)))
SHIFT = 12.0
MAGIC = float(1.5 * 2.0 ** 23)
EPS = 1e-5


def build_nc(phases=4):
    nc = bass.Bass("TRN2", target_bir_lowering=False)

    x = nc.dram_tensor("x", [T, D], F32, kind="ExternalInput").ap()
    wq = nc.dram_tensor("wq", [32, 128, NK * 128], F32, kind="ExternalInput").ap()
    wk = nc.dram_tensor("wk", [16, 128, NK * 128], F32, kind="ExternalInput").ap()
    wv = nc.dram_tensor("wv", [NVB, 128, NK * 256], F32, kind="ExternalInput").ap()
    wf = nc.dram_tensor("wf", [NFB, 128, NK * 256], I8, kind="ExternalInput").ap()
    bq = nc.dram_tensor("bq", [128, 32], F32, kind="ExternalInput").ap()
    bk = nc.dram_tensor("bk", [128, 16], F32, kind="ExternalInput").ap()
    bv_rep = nc.dram_tensor("bv_rep", [128, KVD], F32, kind="ExternalInput").ap()
    ws_rep = nc.dram_tensor("ws_rep", [128, D], F32, kind="ExternalInput").ap()
    mos_rep = nc.dram_tensor("mos_rep", [128, 1], F32, kind="ExternalInput").ap()
    out = nc.dram_tensor("out", [T, D], F32, kind="ExternalOutput").ap()

    s_xnT = nc.dram_tensor("s_xnT", [128, NT * NK * 128], BF16, kind="Internal").ap()
    s_xqT = nc.dram_tensor("s_xqT", [128, NT * NK * 128], BF16, kind="Internal").ap()
    s_qT = nc.dram_tensor("s_qT", [D, T], BF16, kind="Internal").ap()
    s_kT = nc.dram_tensor("s_kT", [KVD, T], BF16, kind="Internal").ap()
    s_v = nc.dram_tensor("s_v", [T, KVD], BF16, kind="Internal").ap()
    s_par = nc.dram_tensor("s_par", [T, D], F32, kind="Internal").ap()
    s_at = nc.dram_tensor("s_at", [T, D], BF16, kind="Internal").ap()
    xnT3 = s_xnT.rearrange("p (tt kc m) -> p tt kc m", tt=NT, kc=NK)
    xqT3 = s_xqT.rearrange("p (tt kc m) -> p tt kc m", tt=NT, kc=NK)

    with ExitStack() as es:
        ec = es.enter_context
        cpool = ec(nc.sbuf_tensor("cpool", [128, 4], F32))
        mos_sb = ec(nc.sbuf_tensor("mos_sb", [128, 1], F32))
        xs_all = ec(nc.sbuf_tensor("xs_all", [128, NT], F32))
        iota32 = ec(nc.sbuf_tensor("iota32", [128, 128], I32))
        mask128 = ec(nc.sbuf_tensor("mask128", [128, 128], BF16))
        masku = ec(nc.sbuf_tensor("masku", [128, 128], BF16))
        ident = ec(nc.sbuf_tensor("ident", [128, 128], BF16))

        sem = {}
        for name in ["mos",
                     "x1s0", "x1s1",
                     "a1", "d1", "p1", "e1",
                     "st1s0", "st1s1", "st1s2", "st1s3",
                     "w2s0", "w2s1", "l2", "m2", "e2",
                     "st2s0", "st2s1",
                     "w2vs0", "w2vs1", "m2v", "e2v",
                     "st2vs0", "st2vs1", "st2vs2", "st2vs3",
                     "w3s0", "w3s1", "l3",
                     "x3s0", "x3s1", "m3", "a3", "e3",
                     "st3s0", "st3s1",
                     "pt4", "kt4s0", "kt4s1", "v4s0", "v4s1", "qt4s0", "qt4s1",
                     "at4s0", "at4s1", "m4", "a4", "d4",
                     "p5a0", "p5a1", "p5b0", "p5b1", "wsc0", "wsc1", "e5", "so"]:
            sem[name] = ec(nc.semaphore("s_" + name))

        block = ec(nc.Block())

        ST1_DONE = [("st1s%d" % j, 16 * 8) for j in range(4)]
        ST2_DONE = [("st2s%d" % j, 16 * 24) for j in range(2)]
        ST2V_DONE = [("st2vs%d" % j, 16 * 32) for j in range(4)]
        ST3_DONE = [("st3s%d" % j, 16 * 128) for j in range(2)]

        def wait_all(eng, pairs):
            for name, val in pairs:
                eng.wait_ge(sem[name], val)

        def drain(pairs):
            @block.sync
            def _(sync):
                wait_all(sync, pairs)

        @block.gpsimd
        def _(g):
            g.iota(iota32[:], pattern=[[-1, 128]], base=0, channel_multiplier=1).then_inc(sem["d1"], 1)

        @block.sync
        def _(sync):
            sync.dma_start(out=mos_sb[:], in_=mos_rep[:]).then_inc(sem["mos"], 16)

        @block.vector
        def _(v):
            v.wait_ge(sem["d1"], 1)
            v.memset(cpool[:, 0:1], SCALE)
            v.memset(cpool[:, 1:2], -SHIFT)
            v.memset(cpool[:, 2:3], EPS)
            v.tensor_single_scalar(mask128[:], iota32[:], 0.0, ALU.is_le)
            v.tensor_single_scalar(masku[:], iota32[:], 0.0, ALU.is_lt)
            v.wait_ge(sem["d1"], 1)
            v.tensor_scalar(masku[:], masku[:], -1.0e9, None, ALU.mult)
            v.tensor_single_scalar(ident[:], iota32[:], 0.0, ALU.is_equal).then_inc(sem["d1"], 1)

        SET_D1 = 2
        DP = 16

        def d1(i, k):
            return SET_D1 + i * DP + k

        with ExitStack() as p1s:
            pc1 = p1s.enter_context
            x_t = [pc1(nc.sbuf_tensor(f"x_t{i}", [128, D], F32)) for i in range(2)]
            xn_bf = [pc1(nc.sbuf_tensor(f"xn_bf{i}", [128, D], BF16)) for i in range(2)]
            xq_bf = [pc1(nc.sbuf_tensor(f"xq_bf{i}", [128, D], BF16)) for i in range(2)]
            qf = pc1(nc.sbuf_tensor("qf", [128, D], F32))
            qi8 = [pc1(nc.sbuf_tensor(f"qi8_{i}", [128, D], I8)) for i in range(2)]
            jnk_bf = pc1(nc.sbuf_tensor("jnk_bf", [128, D], BF16))
            stat = pc1(nc.sbuf_tensor("stat", [128, 2, 10], F32))
            xs16 = pc1(nc.sbuf_tensor("xs16", [128, 2], FP16))
            strip = [pc1(nc.sbuf_tensor(f"strip{i}", [128, NK, 128], BF16)) for i in range(4)]
            tp_ps = [pc1(nc.psum_tensor(f"tp_ps{i}", [128, 1024], BF16)) for i in range(2)]

            def emit_strip_store(sync, i):
                for half in range(2):
                    slot = 2 * (i % 2) + half
                    sync.wait_ge(sem["e1"], 8 * i + 4 * half + 4)
                    dst = xnT3 if half == 0 else xqT3
                    sync.dma_start(out=dst[:, i, :, :],
                                   in_=strip[slot][:]).then_inc(sem["st1s%d" % slot], 16)

            @block.sync
            def _(sync):
                for i in range(NT):
                    if i >= 2:
                        sync.wait_ge(sem["d1"], d1(i - 2, 13))
                        sync.wait_ge(sem["a1"], 4 * (i - 2) + 2)
                    sync.dma_start(out=x_t[i % 2][:],
                                   in_=x[i * 128:(i + 1) * 128, :]).then_inc(sem["x1s%d" % (i % 2)], 16)
                    if i >= 2:
                        emit_strip_store(sync, i - 2)
                for i in range(NT - 2, NT):
                    emit_strip_store(sync, i)

            @block.scalar
            def _(s):
                for i in range(NT):
                    p = i % 2
                    s.wait_ge(sem["x1s%d" % p], 16 * (i // 2 + 1))
                    s.activation(jnk_bf[:], x_t[p][:], AF.Copy, accum_out=stat[:, p, 0:1])
                    s.activation(jnk_bf[:], x_t[p][:], AF.Square, accum_out=stat[:, p, 1:2]).then_inc(sem["a1"], 2)
                    s.wait_ge(sem["d1"], d1(i, 4))
                    s.activation(stat[:, p, 5:6], stat[:, p, 4:5], AF.Sqrt, bias=cpool[:, 2:3]).then_inc(sem["a1"], 1)
                    s.wait_ge(sem["d1"], d1(i, 16))
                    if i >= 2:
                        s.wait_ge(sem["p1"], 8 * (i - 2) + 8)
                    s.copy(xq_bf[p][:], qi8[p][:]).then_inc(sem["a1"], 1)

            @block.vector
            def _(v):
                dcnt = [SET_D1]

                def op(ins):
                    ins.then_inc(sem["d1"], 1)
                    dcnt[0] += 1

                def sw():
                    v.wait_ge(sem["d1"], dcnt[0])

                for i in range(NT):
                    p = i % 2
                    c = stat[:, p, :]
                    assert dcnt[0] == d1(i, 0)
                    v.wait_ge(sem["a1"], 4 * i + 2)
                    op(v.tensor_scalar(c[:, 2:3], c[:, 0:1], 1.0 / D, None, ALU.mult))
                    op(v.tensor_scalar(c[:, 3:4], c[:, 1:2], 1.0 / D, None, ALU.mult))
                    sw()
                    op(v.tensor_tensor(c[:, 4:5], c[:, 2:3], c[:, 2:3], ALU.mult))
                    sw()
                    op(v.tensor_tensor(c[:, 4:5], c[:, 3:4], c[:, 4:5], ALU.subtract))
                    v.wait_ge(sem["a1"], 4 * i + 3)
                    op(v.reciprocal(c[:, 6:7], c[:, 5:6]))
                    if i >= 2:
                        v.wait_ge(sem["p1"], 8 * (i - 2) + 4)
                    sw()
                    op(v.tensor_scalar(xn_bf[p][:], x_t[p][:], c[:, 2:3], c[:, 6:7],
                                       ALU.subtract, ALU.mult))
                    op(v.reduce_max(c[:, 7:8], xn_bf[p][:], AXX, apply_absolute_value=True))
                    sw()
                    op(v.tensor_scalar(c[:, 7:8], c[:, 7:8], 1.0 / 127.0, None, ALU.mult))
                    sw()
                    op(v.tensor_copy(xs16[:, p:p + 1], c[:, 7:8]))
                    sw()
                    op(v.tensor_copy(xs_all[:, i:i + 1], xs16[:, p:p + 1]))
                    op(v.reciprocal(c[:, 8:9], c[:, 7:8]))
                    sw()
                    op(v.tensor_tensor(c[:, 9:10], c[:, 6:7], c[:, 8:9], ALU.mult))
                    sw()
                    op(v.tensor_scalar(qf[:], x_t[p][:], c[:, 2:3], c[:, 9:10],
                                       ALU.subtract, ALU.mult))
                    op(v.tensor_scalar(qf[:], qf[:], MAGIC, None, ALU.add))
                    op(v.tensor_scalar(qf[:], qf[:], -MAGIC, None, ALU.add))
                    op(v.tensor_copy(qi8[p][:], qf[:]))
                    assert dcnt[0] == d1(i, DP)
                    for half in range(2):
                        slot = 2 * p + half
                        for bg in range(4):
                            g_ = 8 * i + 4 * half + bg
                            v.wait_ge(sem["p1"], g_ + 1)
                            if i >= 2 and bg == 0:
                                v.wait_ge(sem["st1s%d" % slot], 16 * (i // 2))
                            v.tensor_copy(strip[slot][:, 8 * bg:8 * bg + 8, :],
                                          tp_ps[g_ % 2][:]).then_inc(sem["e1"], 1)

            @block.tensor
            def _(t):
                for i in range(NT):
                    p = i % 2
                    for half in range(2):
                        src = xn_bf[p] if half == 0 else xq_bf[p]
                        if half == 0:
                            t.wait_ge(sem["d1"], d1(i, 6))
                        else:
                            t.wait_ge(sem["a1"], 4 * i + 4)
                        for bg in range(4):
                            g_ = 8 * i + 4 * half + bg
                            if g_ >= 2:
                                t.wait_ge(sem["e1"], g_ - 1)
                            for j in range(8):
                                col = 8 * bg + j
                                ins = t.matmul(tp_ps[g_ % 2][:, j * 128:(j + 1) * 128],
                                               src[:, col * 128:(col + 1) * 128], ident[:],
                                               is_transpose=True, start=(j == 0), stop=(j == 7))
                            ins.then_inc(sem["p1"], 1)

        if phases < 2:
            drain(ST1_DONE)
            return nc

        with ExitStack() as p2s:
            pc2 = p2s.enter_context
            xnT_sb = pc2(nc.sbuf_tensor("xnT_sb", [128, NT, NK, 128], BF16))
            bq_sb = pc2(nc.sbuf_tensor("bq_sb", [128, 32], F32))
            bk_sb = pc2(nc.sbuf_tensor("bk_sb", [128, 16], F32))
            bv_sb = pc2(nc.sbuf_tensor("bv_sb", [128, KVD], F32))
            wqk_blk = [pc2(nc.sbuf_tensor(f"wqk{i}", [128, NK, 128], BF16)) for i in range(2)]
            qk_stage = [pc2(nc.sbuf_tensor(f"qks{i}", [128, T], BF16)) for i in range(2)]
            wv_blk = [pc2(nc.sbuf_tensor(f"wvb{i}", [128, NK, 256], BF16)) for i in range(2)]
            v_stage = [pc2(nc.sbuf_tensor(f"vst{i}", [128, 256], BF16)) for i in range(4)]

            @block.sync
            def _(sync):
                wait_all(sync, ST1_DONE)
                for q4 in range(4):
                    sync.dma_start(out=xnT_sb[:, q4 * 4:(q4 + 1) * 4, :, :],
                                   in_=xnT3[:, q4 * 4:(q4 + 1) * 4, :, :]).then_inc(sem["l2"], 16)
                sync.dma_start(out=bq_sb[:], in_=bq[:]).then_inc(sem["l2"], 16)
                sync.dma_start(out=bk_sb[:], in_=bk[:]).then_inc(sem["l2"], 16)
                sync.dma_start(out=bv_sb[:], in_=bv_rep[:]).then_inc(sem["l2"], 16)
            L2_TOTAL = 7 * 16

            @block.gpsimd
            def _(g):
                wait_all(g, ST1_DONE)
                for nt in range(N_QK):
                    if nt >= 2:
                        g.wait_ge(sem["m2"], 4 * (nt - 2) + 4)
                    src = wq[nt] if nt < 32 else wk[nt - 32]
                    g.dma_start(out=wqk_blk[nt % 2][:],
                                in_=src.rearrange("p (kc m) -> p kc m", kc=NK)
                                ).then_inc(sem["w2s%d" % (nt % 2)], 16)
                for vb in range(NVB):
                    if vb >= 2:
                        g.wait_ge(sem["m2v"], 16 * (vb - 1))
                    g.dma_start(out=wv_blk[vb % 2][:],
                                in_=wv[vb].rearrange("p (kc m) -> p kc m", kc=NK)
                                ).then_inc(sem["w2vs%d" % (vb % 2)], 16)

            with ExitStack() as p2qk:
                qk_ps = [p2qk.enter_context(nc.psum_tensor(f"qk_ps{i}", [128, T], F32)) for i in range(2)]

                @block.tensor
                def _(t):
                    t.wait_ge(sem["l2"], L2_TOTAL)
                    for nt in range(N_QK):
                        t.wait_ge(sem["w2s%d" % (nt % 2)], 16 * (nt // 2 + 1))
                        for tc in range(4):
                            if nt >= 2:
                                t.wait_ge(sem["e2"], 4 * (nt - 2) + tc + 1)
                            for k in range(NK):
                                ins = t.matmul(qk_ps[nt % 2][:, tc * 512:(tc + 1) * 512],
                                               wqk_blk[nt % 2][:, k, :],
                                               xnT_sb[:, 4 * tc:4 * (tc + 1), k, :],
                                               start=(k == 0), stop=(k == NK - 1))
                            ins.then_inc(sem["m2"], 1)

                @block.vector
                def _(v):
                    for nt in range(N_QK):
                        for tc in range(4):
                            v.wait_ge(sem["m2"], 4 * nt + tc + 1)
                            if nt >= 2 and tc == 0:
                                v.wait_ge(sem["st2s%d" % (nt % 2)], 16 * (nt // 2))
                            bias = bq_sb[:, nt:nt + 1] if nt < 32 else bk_sb[:, nt - 32:nt - 31]
                            v.tensor_scalar(qk_stage[nt % 2][:, tc * 512:(tc + 1) * 512],
                                            qk_ps[nt % 2][:, tc * 512:(tc + 1) * 512],
                                            bias, None, ALU.add).then_inc(sem["e2"], 1)

                @block.sync
                def _(sync):
                    for nt in range(N_QK):
                        sync.wait_ge(sem["e2"], 4 * nt + 4)
                        dst = s_qT[nt * 128:(nt + 1) * 128, :] if nt < 32 else \
                            s_kT[(nt - 32) * 128:(nt - 31) * 128, :]
                        sync.dma_start(out=dst, in_=qk_stage[nt % 2][:]
                                       ).then_inc(sem["st2s%d" % (nt % 2)], 16)

            v_ps = [pc2(nc.psum_tensor(f"v_ps{i}", [128, 512], F32)) for i in range(4)]

            @block.tensor
            def _(t):
                t.wait_ge(sem["e2"], 4 * N_QK)
                for vb in range(NVB):
                    t.wait_ge(sem["w2vs%d" % (vb % 2)], 16 * (vb // 2 + 1))
                    for tt in range(NT):
                        j = vb * NT + tt
                        if j >= 4:
                            t.wait_ge(sem["e2v"], j - 3)
                        for k in range(NK):
                            ins = t.matmul(v_ps[j % 4][:, 0:256], xnT_sb[:, tt, k, :],
                                           wv_blk[vb % 2][:, k, :],
                                           start=(k == 0), stop=(k == NK - 1))
                        ins.then_inc(sem["m2v"], 1)

            @block.vector
            def _(v):
                for vb in range(NVB):
                    for tt in range(NT):
                        j = vb * NT + tt
                        v.wait_ge(sem["m2v"], j + 1)
                        if j >= 4:
                            v.wait_ge(sem["st2vs%d" % (j % 4)], 16 * (j // 4))
                        v.tensor_tensor(v_stage[j % 4][:], v_ps[j % 4][:, 0:256],
                                        bv_sb[:, vb * 256:(vb + 1) * 256], ALU.add).then_inc(sem["e2v"], 1)

            @block.sync
            def _(sync):
                for vb in range(NVB):
                    for tt in range(NT):
                        j = vb * NT + tt
                        sync.wait_ge(sem["e2v"], j + 1)
                        sync.dma_start(out=s_v[tt * 128:(tt + 1) * 128, vb * 256:(vb + 1) * 256],
                                       in_=v_stage[j % 4][:]).then_inc(sem["st2vs%d" % (j % 4)], 16)

        if phases < 3:
            drain(ST2_DONE + ST2V_DONE)
            return nc

        NJ = NFB * NT

        m4_sc, m4_pv, a4_exp, d4_mrg = {}, {}, {}, {}
        a4 = d4 = 0
        d4 += 2
        D4_SETUP = d4
        ei_of = {}
        work = []
        pairs = []
        for c in range(NCQ):
            ktl = 4 * (c + 1)
            for g in range(16):
                for hi in range(2):
                    pairs.append((c, g, hi))
                    for lt in range(ktl):
                        work.append((c, g, hi, lt))
                        a4 += 1
                        a4_exp[(c, g, hi, lt)] = a4
                        ei_of[(c, g, hi, lt)] = a4 - 1
                    d4 += 9
                    d4_mrg[(c, g, hi)] = d4
        pe_order = [("sc", 0)]
        for i in range(len(work)):
            if i + 1 < len(work):
                pe_order.append(("sc", i + 1))
            pe_order.append(("pv", i))
        m4 = 0
        for kind, i in pe_order:
            m4 += 1
            if kind == "sc":
                m4_sc[work[i]] = m4
            else:
                m4_pv[work[i]] = m4

        NI = len(pe_order)
        win = [(NI * t // NJ, NI * (t + 1) // NJ) for t in range(NJ)]
        tick_of_item = np.zeros(NI, dtype=int)
        for t, (a, b) in enumerate(win):
            tick_of_item[a:b] = t
        first_sc_item = {}
        last_sc_item = {}
        last_pv_item = {}
        for idx, (kind, i) in enumerate(pe_order):
            if kind == "sc":
                c, g, hi, lt = work[i]
                if lt == 0:
                    first_sc_item[(c * 16 + g) * 2 + hi] = idx
                last_sc_item[(c * 16 + g) * 2 + hi] = idx
            if kind == "pv":
                c, g, hi, lt = work[i]
                if lt == 4 * (c + 1) - 1:
                    last_pv_item[(c * 16 + g) * 2 + hi] = idx
        load_tick = {}
        for pi in range(len(pairs)):
            c, g, hi = pairs[pi]
            cg = c * 16 + g
            lt_ = int(tick_of_item[first_sc_item[pi]]) - 2
            if hi == 0 and cg >= 2:
                lt_ = max(lt_, int(tick_of_item[last_pv_item[2 * cg - 3]]) + 1)
            if pi >= 2:
                lt_ = max(lt_, int(tick_of_item[last_sc_item[pi - 2]]) + 1)
            lt_ = max(0, lt_)
            assert lt_ <= int(tick_of_item[first_sc_item[pi]]), (pi, lt_)
            load_tick[pi] = lt_
        mrg_tick = {pi: int(tick_of_item[last_pv_item[pi]]) for pi in range(len(pairs))}
        atst_tick = {pi: min(NJ - 1, mrg_tick[pi] + 1) for pi in range(len(pairs))}
        loads_at = [[] for _ in range(NJ)]
        mrg_at = [[] for _ in range(NJ)]
        atst_at = [[] for _ in range(NJ)]
        for pi in range(len(pairs)):
            loads_at[load_tick[pi]].append(pi)
            mrg_at[mrg_tick[pi]].append(pi)
            atst_at[atst_tick[pi]].append(pi)
        exps_at = [[] for _ in range(NJ)]
        for idx, (kind, i) in enumerate(pe_order):
            if kind == "sc":
                exps_at[int(tick_of_item[idx])].append(i)

        def st2_wait_upto(eng, nt_max):
            for par in range(2):
                cnt = sum(1 for n in range(nt_max + 1) if n % 2 == par)
                if cnt:
                    eng.wait_ge(sem["st2s%d" % par], 16 * cnt)

        with ExitStack() as p34:
            pc3 = p34.enter_context
            xqT_sb = pc3(nc.sbuf_tensor("xqT_sb", [128, NT, NK, 128], BF16))
            ws_ch = [pc3(nc.sbuf_tensor(f"wsch{i}", [128, 256], F32)) for i in range(2)]
            wf_blk = [pc3(nc.sbuf_tensor(f"wfb{i}", [128, NK, 256], BF16)) for i in range(2)]
            y16 = [pc3(nc.sbuf_tensor(f"y16_{i}", [128, 256], FP16)) for i in range(2)]
            x_strip = [pc3(nc.sbuf_tensor(f"xs3_{i}", [128, 256], F32)) for i in range(2)]
            par_stage = [pc3(nc.sbuf_tensor(f"par{i}", [128, 256], F32)) for i in range(2)]
            f_ps = [pc3(nc.psum_tensor(f"f_ps{i}", [128, 512], F32)) for i in range(2)]
            kt_sb = [pc3(nc.sbuf_tensor(f"kt{i}", [128, T], BF16)) for i in range(2)]
            v_sb = [pc3(nc.sbuf_tensor(f"v4_{i}", [128, NT, 144], BF16)) for i in range(2)]
            qt_sb = [pc3(nc.sbuf_tensor(f"qt{i}", [128, 512], BF16)) for i in range(2)]
            e_sb = [pc3(nc.sbuf_tensor(f"e{i}", [128, 512], BF16)) for i in range(2)]
            sums4 = pc3(nc.sbuf_tensor("sums4", [128, 4], F32))
            rec4 = pc3(nc.sbuf_tensor("rec4", [128, 4], F32))
            at_st = [pc3(nc.sbuf_tensor(f"at_st{i}", [128, 4, 128], BF16)) for i in range(2)]
            sc_ps = [pc3(nc.psum_tensor(f"sc_ps{i}", [128, 512], F32)) for i in range(2)]
            pv_ps = [pc3(nc.psum_tensor(f"pv_ps{i}", [128, 512], F32)) for i in range(4)]

            @block.gpsimd
            def _(g):
                wait_all(g, ST2_DONE + ST2V_DONE)
                for fb in range(NFB):
                    if fb >= 2:
                        g.wait_ge(sem["m3"], NT * (fb - 1))
                    g.dma_start(out=wf_blk[fb % 2][:],
                                in_=wf[fb].rearrange("p (kc m) -> p kc m", kc=NK)
                                ).then_inc(sem["w3s%d" % (fb % 2)], 16)

            def emit_p4_loads(sync, pi):
                c, g, hi = pairs[pi]
                cg = c * 16 + g
                ktl = 4 * (c + 1)
                kmax = 512 * (c + 1)
                if hi == 0:
                    if cg >= 2:
                        pc_, pg_ = (cg - 2) // 16, (cg - 2) % 16
                        sync.wait_ge(sem["m4"], m4_pv[(pc_, pg_, 1, 4 * (pc_ + 1) - 1)])
                    else:
                        st2_wait_upto(sync, 32 + g)
                        for sl in range(4):
                            sync.wait_ge(sem["st2vs%d" % sl], 64 * (g // 2 + 1))
                    sync.dma_start(out=kt_sb[g % 2][:, 0:kmax],
                                   in_=s_kT[g * 128:(g + 1) * 128, 0:kmax]
                                   ).then_inc(sem["kt4s%d" % (g % 2)], 16)
                    sync.dma_start(out=v_sb[g % 2][:, 0:ktl, 0:128],
                                   in_=s_v[0:kmax, g * 128:(g + 1) * 128]
                                   .rearrange("(n p) d -> p n d", p=128)
                                   ).then_inc(sem["v4s%d" % (g % 2)], 16)
                h = g + 16 * hi
                if pi >= 2:
                    pcg, phi = (pi - 2) // 2, (pi - 2) % 2
                    pc_, pg_ = pcg // 16, pcg % 16
                    sync.wait_ge(sem["m4"], m4_sc[(pc_, pg_, phi, 4 * (pc_ + 1) - 1)])
                else:
                    st2_wait_upto(sync, h)
                sync.dma_start(out=qt_sb[pi % 2][:],
                               in_=s_qT[h * 128:(h + 1) * 128, 512 * c:512 * (c + 1)]
                               ).then_inc(sem["qt4s%d" % (pi % 2)], 16)

            def emit_at_store(sync, pi):
                c, g, hi = pairs[pi]
                h = g + 16 * hi
                sync.wait_ge(sem["d4"], d4_mrg[(c, g, hi)])
                sync.dma_start(out=s_at[512 * c:512 * (c + 1), h * 128:(h + 1) * 128]
                               .rearrange("(n p) d -> p n d", p=128),
                               in_=at_st[pi % 2][:]).then_inc(sem["at4s%d" % (pi % 2)], 16)

            @block.sync
            def _(sync):
                wait_all(sync, ST1_DONE + ST2_DONE + ST2V_DONE)
                for q4 in range(4):
                    sync.dma_start(out=xqT_sb[:, q4 * 4:(q4 + 1) * 4, :, :],
                                   in_=xqT3[:, q4 * 4:(q4 + 1) * 4, :, :]).then_inc(sem["l3"], 16)
                for t in range(NJ):
                    fb, tt = t // NT, t % NT
                    j = t
                    if tt == 0:
                        if fb >= 2:
                            sync.wait_ge(sem["e3"], 2 * ((fb - 1) * NT - 1) + 1)
                        sync.dma_start(out=ws_ch[fb % 2][:],
                                       in_=ws_rep[:, fb * 256:(fb + 1) * 256]
                                       ).then_inc(sem["wsc%d" % (fb % 2)], 16)
                    if j >= 2:
                        sync.wait_ge(sem["e3"], 2 * (j - 2) + 2)
                    sync.dma_start(out=x_strip[j % 2][:],
                                   in_=x[tt * 128:(tt + 1) * 128, fb * 256:(fb + 1) * 256]
                                   ).then_inc(sem["x3s%d" % (j % 2)], 16)
                    if j >= 2:
                        jj = j - 2
                        fb2, tt2 = jj // NT, jj % NT
                        sync.wait_ge(sem["e3"], 2 * jj + 2)
                        sync.dma_start(out=s_par[tt2 * 128:(tt2 + 1) * 128, fb2 * 256:(fb2 + 1) * 256],
                                       in_=par_stage[jj % 2][:]).then_inc(sem["st3s%d" % (jj % 2)], 16)
                    for pi in loads_at[t]:
                        emit_p4_loads(sync, pi)
                    for pi in atst_at[t]:
                        emit_at_store(sync, pi)
                for jj in range(NJ - 2, NJ):
                    fb2, tt2 = jj // NT, jj % NT
                    sync.wait_ge(sem["e3"], 2 * jj + 2)
                    sync.dma_start(out=s_par[tt2 * 128:(tt2 + 1) * 128, fb2 * 256:(fb2 + 1) * 256],
                                   in_=par_stage[jj % 2][:]).then_inc(sem["st3s%d" % (jj % 2)], 16)
            L3_TOTAL = 4 * 16

            @block.tensor
            def _(t):
                def emit_scores2(i):
                    c, g, hi, lt = work[i]
                    cg = c * 16 + g
                    pi = cg * 2 + hi
                    if lt == 0:
                        t.wait_ge(sem["kt4s%d" % (g % 2)], 16 * (cg // 2 + 1))
                        t.wait_ge(sem["v4s%d" % (g % 2)], 16 * (cg // 2 + 1))
                        t.wait_ge(sem["qt4s%d" % (pi % 2)], 16 * (pi // 2 + 1))
                    if i >= 2:
                        t.wait_ge(sem["a4"], i - 1)
                    diag = lt >= 4 * c
                    ins = t.matmul(sc_ps[i % 2][:], kt_sb[g % 2][:, lt * 128:(lt + 1) * 128],
                                   qt_sb[pi % 2][:], start=True, stop=not diag)
                    if diag:
                        q0 = (lt - 4 * c) * 128
                        ins = t.matmul(sc_ps[i % 2][:, q0:q0 + 128], masku[:], ident[:],
                                       start=False, stop=True)
                    ins.then_inc(sem["m4"], 1)

                def emit_pv(i):
                    c, g, hi, lt = work[i]
                    t.wait_ge(sem["a4"], a4_exp[(c, g, hi, lt)])
                    if lt == 0:
                        pi = (c * 16 + g) * 2 + hi
                        if pi >= 1:
                            pcg, phi = (pi - 1) // 2, (pi - 1) % 2
                            t.wait_ge(sem["d4"], d4_mrg[(pcg // 16, pcg % 16, phi)])
                        else:
                            t.wait_ge(sem["d4"], D4_SETUP)
                    for qv in range(max(0, lt - 4 * c), 4):
                        ins = t.matmul(pv_ps[qv][:, 0:129],
                                       e_sb[(a4_exp[(c, g, hi, lt)] - 1) % 2][:, qv * 128:(qv + 1) * 128],
                                       v_sb[g % 2][:, lt, 0:129],
                                       start=(lt == 0), stop=(lt == 4 * c + qv))
                    ins.then_inc(sem["m4"], 1)

                t.wait_ge(sem["l3"], L3_TOTAL)
                t.wait_ge(sem["e2v"], NVB * NT)
                for t_ in range(NJ):
                    fb, tt = t_ // NT, t_ % NT
                    j = t_
                    t.wait_ge(sem["w3s%d" % (fb % 2)], 16 * (fb // 2 + 1))
                    if j >= 2:
                        t.wait_ge(sem["a3"], j - 1)
                    for k in range(NK):
                        ins = t.matmul(f_ps[j % 2][:, 0:256], xqT_sb[:, tt, k, :],
                                       wf_blk[fb % 2][:, k, :],
                                       start=(k == 0), stop=(k == NK - 1))
                    ins.then_inc(sem["m3"], 1)
                    a, b = win[t_]
                    for idx in range(a, b):
                        kind, i = pe_order[idx]
                        if kind == "sc":
                            emit_scores2(i)
                        else:
                            emit_pv(i)

            @block.scalar
            def _(s):
                s.wait_ge(sem["mos"], 16)
                for t_ in range(NJ):
                    fb, tt = t_ // NT, t_ % NT
                    j = t_
                    s.wait_ge(sem["m3"], j + 1)
                    if j >= 2:
                        s.wait_ge(sem["e3"], 2 * (j - 2) + 1)
                    s.activation(y16[j % 2][:], f_ps[j % 2][:, 0:256], AF.Copy,
                                 scale=mos_sb[:, 0:1]).then_inc(sem["a3"], 1)
                    for i in exps_at[t_]:
                        c, g, hi, lt = work[i]
                        ei = ei_of[(c, g, hi, lt)]
                        s.wait_ge(sem["m4"], m4_sc[(c, g, hi, lt)])
                        if ei >= 2:
                            s.wait_ge(sem["m4"], m4_pv[work[ei - 2]])
                        q0 = max(0, (lt - 4 * c) * 128)
                        s.activation(e_sb[ei % 2][:, q0:512], sc_ps[ei % 2][:, q0:512],
                                     AF.Exp, bias=cpool[:, 1:2],
                                     scale=cpool[:, 0:1]).then_inc(sem["a4"], 1)

            @block.vector
            def _(v):
                d4c = 2
                wait_all(v, ST2_DONE + ST2V_DONE)
                v.memset(v_sb[0][:, :, 128:129], 1.0).then_inc(sem["d4"], 1)
                v.memset(v_sb[1][:, :, 128:129], 1.0).then_inc(sem["d4"], 1)
                for t_ in range(NJ):
                    fb, tt = t_ // NT, t_ % NT
                    j = t_
                    v.wait_ge(sem["a3"], j + 1)
                    v.wait_ge(sem["x3s%d" % (j % 2)], 16 * (j // 2 + 1))
                    v.wait_ge(sem["wsc%d" % (fb % 2)], 16 * (fb // 2 + 1))
                    if j >= 2:
                        v.wait_ge(sem["st3s%d" % (j % 2)], 16 * (j // 2))
                    v.scalar_tensor_tensor(par_stage[j % 2][:], y16[j % 2][:],
                                           xs_all[:, tt:tt + 1], ws_ch[fb % 2][:],
                                           ALU.mult, ALU.mult).then_inc(sem["e3"], 1)
                    v.tensor_tensor(par_stage[j % 2][:], par_stage[j % 2][:],
                                    x_strip[j % 2][:], ALU.add).then_inc(sem["e3"], 1)
                    for pi in mrg_at[t_]:
                        c, g, hi = pairs[pi]
                        ktl = 4 * (c + 1)
                        v.wait_ge(sem["m4"], m4_pv[(c, g, hi, ktl - 1)])
                        if pi >= 2:
                            v.wait_ge(sem["at4s%d" % (pi % 2)], 16 * (pi // 2))
                        for qv in range(4):
                            v.tensor_copy(sums4[:, qv:qv + 1], pv_ps[qv][:, 128:129]).then_inc(sem["d4"], 1)
                            d4c += 1
                        v.wait_ge(sem["d4"], d4c)
                        v.reciprocal(rec4[:], sums4[:]).then_inc(sem["d4"], 1)
                        d4c += 1
                        v.wait_ge(sem["d4"], d4c)
                        for qv in range(4):
                            v.tensor_scalar(at_st[pi % 2][:, qv, :], pv_ps[qv][:, 0:128],
                                            rec4[:, qv:qv + 1], None, ALU.mult).then_inc(sem["d4"], 1)
                            d4c += 1
                        assert d4c == d4_mrg[(c, g, hi)]

        AT4_DONE = [("at4s%d" % jj, 16 * 64) for jj in range(2)]
        with ExitStack() as p5s:
            pc5 = p5s.enter_context
            p5_par = [pc5(nc.sbuf_tensor(f"p5p{i}", [128, D], F32)) for i in range(2)]
            p5_at = [pc5(nc.sbuf_tensor(f"p5a{i}", [128, D], BF16)) for i in range(2)]
            p5_out = [pc5(nc.sbuf_tensor(f"p5o{i}", [128, D], F32)) for i in range(2)]

            @block.sync
            def _(sync):
                wait_all(sync, ST3_DONE + AT4_DONE)
                for i in range(NT):
                    if i >= 2:
                        sync.wait_ge(sem["e5"], i - 1)
                    sync.dma_start(out=p5_par[i % 2][:],
                                   in_=s_par[i * 128:(i + 1) * 128, :]).then_inc(sem["p5a%d" % (i % 2)], 16)
                    sync.dma_start(out=p5_at[i % 2][:],
                                   in_=s_at[i * 128:(i + 1) * 128, :]).then_inc(sem["p5b%d" % (i % 2)], 16)
                    if i >= 2:
                        sync.dma_start(out=out[(i - 2) * 128:(i - 1) * 128, :],
                                       in_=p5_out[i % 2][:]).then_inc(sem["so"], 16)
                for i in range(NT - 2, NT):
                    sync.wait_ge(sem["e5"], i + 1)
                    sync.dma_start(out=out[i * 128:(i + 1) * 128, :],
                                   in_=p5_out[i % 2][:]).then_inc(sem["so"], 16)
                sync.wait_ge(sem["so"], 16 * NT)

            @block.vector
            def _(v):
                for i in range(NT):
                    v.wait_ge(sem["p5a%d" % (i % 2)], 16 * (i // 2 + 1))
                    v.wait_ge(sem["p5b%d" % (i % 2)], 16 * (i // 2 + 1))
                    if i >= 2:
                        v.wait_ge(sem["so"], 16 * (i - 1))
                    v.tensor_tensor(p5_out[i % 2][:], p5_par[i % 2][:],
                                    p5_at[i % 2][:], ALU.add).then_inc(sem["e5"], 1)

    return nc


_NC_CACHE = {}


def kernel(L0_attn_x, L0_attn_Wq, L0_attn_bq, L0_attn_Wk, L0_attn_bk,
           L0_attn_Wv, L0_attn_bv, L0_ffn_w_int8, L0_ffn_weight_scale,
           L0_ffn_matmul_out_scale):
    x = np.asarray(L0_attn_x)
    f32 = np.float32

    def img_qk(W, ntile):
        W = np.asarray(W, dtype=f32)
        return np.ascontiguousarray(
            W.reshape(ntile, 128, NK, 128).transpose(0, 3, 2, 1).reshape(ntile, 128, NK * 128))

    def img_256(W, nb, dt):
        W = np.asarray(W)
        return np.ascontiguousarray(
            W.reshape(nb, 256, NK, 128).transpose(0, 3, 2, 1).reshape(nb, 128, NK * 256)).astype(dt, copy=False)

    wq_img = img_qk(L0_attn_Wq, 32)
    wk_img = img_qk(L0_attn_Wk, 16)
    wv_img = img_256(L0_attn_Wv, NVB, f32)
    wf_img = img_256(L0_ffn_w_int8, NFB, np.int8)
    bq_img = np.ascontiguousarray(np.asarray(L0_attn_bq, dtype=f32).reshape(32, 128).T)
    bk_img = np.ascontiguousarray(np.asarray(L0_attn_bk, dtype=f32).reshape(16, 128).T)
    bv_img = np.ascontiguousarray(np.broadcast_to(np.asarray(L0_attn_bv, dtype=f32)[None, :], (128, KVD)))
    ws_img = np.ascontiguousarray(np.broadcast_to(np.asarray(L0_ffn_weight_scale, dtype=f32)[None, :], (128, D)))
    mos_img = np.ascontiguousarray(np.broadcast_to(np.asarray(L0_ffn_matmul_out_scale, dtype=f32).reshape(1, 1), (128, 1)))

    if "nc" not in _NC_CACHE:
        _NC_CACHE["nc"] = build_nc()
    nc = _NC_CACHE["nc"]

    in_maps = []
    for b in range(8):
        in_maps.append({
            "x": np.ascontiguousarray(x[b], dtype=f32),
            "wq": wq_img, "wk": wk_img, "wv": wv_img, "wf": wf_img,
            "bq": bq_img, "bk": bk_img, "bv_rep": bv_img,
            "ws_rep": ws_img, "mos_rep": mos_img,
        })
    res = run_bass_kernel_spmd(nc, in_maps, core_ids=list(range(8)))
    return np.stack([np.asarray(r["out"]) for r in res.results], axis=0)


# version 09
# speedup vs baseline: 1.3220x; 1.0091x over previous
import numpy as np
import concourse.bass as bass
import concourse.mybir as mybir
from concourse.bass_utils import run_bass_kernel_spmd
from contextlib import ExitStack

F32 = mybir.dt.float32
BF16 = mybir.dt.bfloat16
FP16 = mybir.dt.float16
I8 = mybir.dt.int8
I32 = mybir.dt.int32
AF = mybir.ActivationFunctionType
ALU = mybir.AluOpType
AXX = mybir.AxisListType.X

T = 2048
D = 4096
KVD = 2048
NT = T // 128
NK = D // 128
N_QK = 48
NVB = KVD // 256
NFB = D // 256
NCQ = 4
SCALE = float(1.0 / np.sqrt(np.float32(32.0)))
SHIFT = 12.0
MAGIC = float(1.5 * 2.0 ** 23)
EPS = 1e-5


def build_nc(phases=4):
    nc = bass.Bass("TRN2", target_bir_lowering=False)

    x = nc.dram_tensor("x", [T, D], F32, kind="ExternalInput").ap()
    wq = nc.dram_tensor("wq", [32, 128, NK * 128], F32, kind="ExternalInput").ap()
    wk = nc.dram_tensor("wk", [16, 128, NK * 128], F32, kind="ExternalInput").ap()
    wv = nc.dram_tensor("wv", [NVB, 128, NK * 256], F32, kind="ExternalInput").ap()
    wf = nc.dram_tensor("wf", [NFB, 128, NK * 256], I8, kind="ExternalInput").ap()
    bq = nc.dram_tensor("bq", [128, 32], F32, kind="ExternalInput").ap()
    bk = nc.dram_tensor("bk", [128, 16], F32, kind="ExternalInput").ap()
    bv_rep = nc.dram_tensor("bv_rep", [128, KVD], F32, kind="ExternalInput").ap()
    ws_rep = nc.dram_tensor("ws_rep", [128, D], F32, kind="ExternalInput").ap()
    mos_rep = nc.dram_tensor("mos_rep", [128, 1], F32, kind="ExternalInput").ap()
    out = nc.dram_tensor("out", [T, D], F32, kind="ExternalOutput").ap()

    s_xnT = nc.dram_tensor("s_xnT", [128, NT * NK * 128], BF16, kind="Internal").ap()
    s_xqT = nc.dram_tensor("s_xqT", [128, NT * NK * 128], BF16, kind="Internal").ap()
    s_qT = nc.dram_tensor("s_qT", [D, T], BF16, kind="Internal").ap()
    s_kT = nc.dram_tensor("s_kT", [KVD, T], BF16, kind="Internal").ap()
    s_v = nc.dram_tensor("s_v", [T, KVD], BF16, kind="Internal").ap()
    s_par = nc.dram_tensor("s_par", [T, D], F32, kind="Internal").ap()
    s_at = nc.dram_tensor("s_at", [T, D], BF16, kind="Internal").ap()
    xnT3 = s_xnT.rearrange("p (tt kc m) -> p tt kc m", tt=NT, kc=NK)
    xqT3 = s_xqT.rearrange("p (tt kc m) -> p tt kc m", tt=NT, kc=NK)

    with ExitStack() as es:
        ec = es.enter_context
        cpool = ec(nc.sbuf_tensor("cpool", [128, 4], F32))
        mos_sb = ec(nc.sbuf_tensor("mos_sb", [128, 1], F32))
        xs_all = ec(nc.sbuf_tensor("xs_all", [128, NT], F32))
        iota32 = ec(nc.sbuf_tensor("iota32", [128, 128], I32))
        mask128 = ec(nc.sbuf_tensor("mask128", [128, 128], BF16))
        masku = ec(nc.sbuf_tensor("masku", [128, 128], BF16))
        ident = ec(nc.sbuf_tensor("ident", [128, 128], BF16))

        sem = {}
        for name in ["mos",
                     "x1s0", "x1s1",
                     "a1", "d1", "p1", "e1",
                     "st1s0", "st1s1", "st1s2", "st1s3",
                     "w2s0", "w2s1", "l2", "m2", "e2",
                     "st2s0", "st2s1",
                     "w2vs0", "w2vs1", "m2v", "e2v",
                     "st2vs0", "st2vs1", "st2vs2", "st2vs3",
                     "w3s0", "w3s1", "l3",
                     "x3s0", "x3s1", "m3", "a3", "e3",
                     "st3s0", "st3s1",
                     "pt4", "kt4s0", "kt4s1", "v4s0", "v4s1", "qt4s0", "qt4s1",
                     "at4s0", "at4s1", "m4", "a4", "d4",
                     "p5a0", "p5a1", "p5b0", "p5b1", "wsc0", "wsc1", "e5", "so"]:
            sem[name] = ec(nc.semaphore("s_" + name))

        block = ec(nc.Block())

        ST1_DONE = [("st1s%d" % j, 16 * 8) for j in range(4)]
        ST2_DONE = [("st2s%d" % j, 16 * 24) for j in range(2)]
        ST2V_DONE = [("st2vs%d" % j, 16 * 32) for j in range(4)]
        ST3_DONE = [("st3s%d" % j, 16 * 128) for j in range(2)]

        def wait_all(eng, pairs):
            for name, val in pairs:
                eng.wait_ge(sem[name], val)

        def drain(pairs):
            @block.sync
            def _(sync):
                wait_all(sync, pairs)

        @block.gpsimd
        def _(g):
            g.iota(iota32[:], pattern=[[-1, 128]], base=0, channel_multiplier=1).then_inc(sem["d1"], 1)

        @block.sync
        def _(sync):
            sync.dma_start(out=mos_sb[:], in_=mos_rep[:]).then_inc(sem["mos"], 16)

        @block.vector
        def _(v):
            v.wait_ge(sem["d1"], 1)
            v.memset(cpool[:, 0:1], SCALE)
            v.memset(cpool[:, 1:2], -SHIFT)
            v.memset(cpool[:, 2:3], EPS)
            v.tensor_single_scalar(mask128[:], iota32[:], 0.0, ALU.is_le)
            v.tensor_single_scalar(masku[:], iota32[:], 0.0, ALU.is_lt)
            v.wait_ge(sem["d1"], 1)
            v.tensor_scalar(masku[:], masku[:], -1.0e9, None, ALU.mult)
            v.tensor_single_scalar(ident[:], iota32[:], 0.0, ALU.is_equal).then_inc(sem["d1"], 1)

        SET_D1 = 2
        DP = 16

        def d1(i, k):
            return SET_D1 + i * DP + k

        with ExitStack() as p1s:
            pc1 = p1s.enter_context
            x_t = [pc1(nc.sbuf_tensor(f"x_t{i}", [128, D], F32)) for i in range(2)]
            xn_bf = [pc1(nc.sbuf_tensor(f"xn_bf{i}", [128, D], BF16)) for i in range(2)]
            xq_bf = [pc1(nc.sbuf_tensor(f"xq_bf{i}", [128, D], BF16)) for i in range(2)]
            qf = pc1(nc.sbuf_tensor("qf", [128, D], F32))
            qi8 = [pc1(nc.sbuf_tensor(f"qi8_{i}", [128, D], I8)) for i in range(2)]
            jnk_bf = pc1(nc.sbuf_tensor("jnk_bf", [128, D], BF16))
            stat = pc1(nc.sbuf_tensor("stat", [128, 2, 10], F32))
            xs16 = pc1(nc.sbuf_tensor("xs16", [128, 2], FP16))
            strip = [pc1(nc.sbuf_tensor(f"strip{i}", [128, NK, 128], BF16)) for i in range(4)]
            tp_ps = [pc1(nc.psum_tensor(f"tp_ps{i}", [128, 1024], BF16)) for i in range(2)]

            def emit_strip_store(sync, i):
                for half in range(2):
                    slot = 2 * (i % 2) + half
                    sync.wait_ge(sem["e1"], 8 * i + 4 * half + 4)
                    dst = xnT3 if half == 0 else xqT3
                    sync.dma_start(out=dst[:, i, :, :],
                                   in_=strip[slot][:]).then_inc(sem["st1s%d" % slot], 16)

            @block.sync
            def _(sync):
                for i in range(NT):
                    if i >= 2:
                        sync.wait_ge(sem["d1"], d1(i - 2, 13))
                        sync.wait_ge(sem["a1"], 4 * (i - 2) + 2)
                    sync.dma_start(out=x_t[i % 2][:],
                                   in_=x[i * 128:(i + 1) * 128, :]).then_inc(sem["x1s%d" % (i % 2)], 16)
                    if i >= 2:
                        emit_strip_store(sync, i - 2)
                for i in range(NT - 2, NT):
                    emit_strip_store(sync, i)

            @block.scalar
            def _(s):
                for i in range(NT):
                    p = i % 2
                    s.wait_ge(sem["x1s%d" % p], 16 * (i // 2 + 1))
                    s.activation(jnk_bf[:], x_t[p][:], AF.Copy, accum_out=stat[:, p, 0:1])
                    s.activation(jnk_bf[:], x_t[p][:], AF.Square, accum_out=stat[:, p, 1:2]).then_inc(sem["a1"], 2)
                    s.wait_ge(sem["d1"], d1(i, 4))
                    s.activation(stat[:, p, 5:6], stat[:, p, 4:5], AF.Sqrt, bias=cpool[:, 2:3]).then_inc(sem["a1"], 1)
                    s.wait_ge(sem["d1"], d1(i, 16))
                    if i >= 2:
                        s.wait_ge(sem["p1"], 8 * (i - 2) + 8)
                    s.copy(xq_bf[p][:], qi8[p][:]).then_inc(sem["a1"], 1)

            @block.vector
            def _(v):
                dcnt = [SET_D1]

                def op(ins):
                    ins.then_inc(sem["d1"], 1)
                    dcnt[0] += 1

                def sw():
                    v.wait_ge(sem["d1"], dcnt[0])

                for i in range(NT):
                    p = i % 2
                    c = stat[:, p, :]
                    assert dcnt[0] == d1(i, 0)
                    v.wait_ge(sem["a1"], 4 * i + 2)
                    op(v.tensor_scalar(c[:, 2:3], c[:, 0:1], 1.0 / D, None, ALU.mult))
                    op(v.tensor_scalar(c[:, 3:4], c[:, 1:2], 1.0 / D, None, ALU.mult))
                    sw()
                    op(v.tensor_tensor(c[:, 4:5], c[:, 2:3], c[:, 2:3], ALU.mult))
                    sw()
                    op(v.tensor_tensor(c[:, 4:5], c[:, 3:4], c[:, 4:5], ALU.subtract))
                    v.wait_ge(sem["a1"], 4 * i + 3)
                    op(v.reciprocal(c[:, 6:7], c[:, 5:6]))
                    if i >= 2:
                        v.wait_ge(sem["p1"], 8 * (i - 2) + 4)
                    sw()
                    op(v.tensor_scalar(xn_bf[p][:], x_t[p][:], c[:, 2:3], c[:, 6:7],
                                       ALU.subtract, ALU.mult))
                    op(v.reduce_max(c[:, 7:8], xn_bf[p][:], AXX, apply_absolute_value=True))
                    sw()
                    op(v.tensor_scalar(c[:, 7:8], c[:, 7:8], 1.0 / 127.0, None, ALU.mult))
                    sw()
                    op(v.tensor_copy(xs16[:, p:p + 1], c[:, 7:8]))
                    sw()
                    op(v.tensor_copy(xs_all[:, i:i + 1], xs16[:, p:p + 1]))
                    op(v.reciprocal(c[:, 8:9], c[:, 7:8]))
                    sw()
                    op(v.tensor_tensor(c[:, 9:10], c[:, 6:7], c[:, 8:9], ALU.mult))
                    sw()
                    op(v.tensor_scalar(qf[:], x_t[p][:], c[:, 2:3], c[:, 9:10],
                                       ALU.subtract, ALU.mult))
                    op(v.tensor_scalar(qf[:], qf[:], MAGIC, None, ALU.add))
                    op(v.tensor_scalar(qf[:], qf[:], -MAGIC, None, ALU.add))
                    op(v.tensor_copy(qi8[p][:], qf[:]))
                    assert dcnt[0] == d1(i, DP)
                    for half in range(2):
                        slot = 2 * p + half
                        for bg in range(4):
                            g_ = 8 * i + 4 * half + bg
                            v.wait_ge(sem["p1"], g_ + 1)
                            if i >= 2 and bg == 0:
                                v.wait_ge(sem["st1s%d" % slot], 16 * (i // 2))
                            v.tensor_copy(strip[slot][:, 8 * bg:8 * bg + 8, :],
                                          tp_ps[g_ % 2][:]).then_inc(sem["e1"], 1)

            @block.tensor
            def _(t):
                for i in range(NT):
                    p = i % 2
                    for half in range(2):
                        src = xn_bf[p] if half == 0 else xq_bf[p]
                        if half == 0:
                            t.wait_ge(sem["d1"], d1(i, 6))
                        else:
                            t.wait_ge(sem["a1"], 4 * i + 4)
                        for bg in range(4):
                            g_ = 8 * i + 4 * half + bg
                            if g_ >= 2:
                                t.wait_ge(sem["e1"], g_ - 1)
                            for j in range(8):
                                col = 8 * bg + j
                                ins = t.matmul(tp_ps[g_ % 2][:, j * 128:(j + 1) * 128],
                                               src[:, col * 128:(col + 1) * 128], ident[:],
                                               is_transpose=True, start=(j == 0), stop=(j == 7))
                            ins.then_inc(sem["p1"], 1)

        if phases < 2:
            drain(ST1_DONE)
            return nc

        with ExitStack() as p2s:
            pc2 = p2s.enter_context
            xnT_sb = pc2(nc.sbuf_tensor("xnT_sb", [128, NT, NK, 128], BF16))
            bq_sb = pc2(nc.sbuf_tensor("bq_sb", [128, 32], F32))
            bk_sb = pc2(nc.sbuf_tensor("bk_sb", [128, 16], F32))
            bv_sb = pc2(nc.sbuf_tensor("bv_sb", [128, KVD], F32))
            wqk_blk = [pc2(nc.sbuf_tensor(f"wqk{i}", [128, NK, 128], BF16)) for i in range(2)]
            qk_stage = [pc2(nc.sbuf_tensor(f"qks{i}", [128, T], BF16)) for i in range(2)]
            wv_blk = [pc2(nc.sbuf_tensor(f"wvb{i}", [128, NK, 256], BF16)) for i in range(2)]
            v_stage = [pc2(nc.sbuf_tensor(f"vst{i}", [128, 256], BF16)) for i in range(4)]

            @block.sync
            def _(sync):
                wait_all(sync, ST1_DONE)
                for q4 in range(4):
                    sync.dma_start(out=xnT_sb[:, q4 * 4:(q4 + 1) * 4, :, :],
                                   in_=xnT3[:, q4 * 4:(q4 + 1) * 4, :, :]).then_inc(sem["l2"], 16)
                sync.dma_start(out=bq_sb[:], in_=bq[:]).then_inc(sem["l2"], 16)
                sync.dma_start(out=bk_sb[:], in_=bk[:]).then_inc(sem["l2"], 16)
                sync.dma_start(out=bv_sb[:], in_=bv_rep[:]).then_inc(sem["l2"], 16)
            L2_TOTAL = 7 * 16

            @block.gpsimd
            def _(g):
                wait_all(g, ST1_DONE)
                for nt in range(N_QK):
                    if nt >= 2:
                        g.wait_ge(sem["m2"], 4 * (nt - 2) + 4)
                    src = wq[nt] if nt < 32 else wk[nt - 32]
                    g.dma_start(out=wqk_blk[nt % 2][:],
                                in_=src.rearrange("p (kc m) -> p kc m", kc=NK)
                                ).then_inc(sem["w2s%d" % (nt % 2)], 16)
                for vb in range(NVB):
                    if vb >= 2:
                        g.wait_ge(sem["m2v"], 16 * (vb - 1))
                    g.dma_start(out=wv_blk[vb % 2][:],
                                in_=wv[vb].rearrange("p (kc m) -> p kc m", kc=NK)
                                ).then_inc(sem["w2vs%d" % (vb % 2)], 16)

            with ExitStack() as p2qk:
                qk_ps = [p2qk.enter_context(nc.psum_tensor(f"qk_ps{i}", [128, T], F32)) for i in range(2)]

                @block.tensor
                def _(t):
                    t.wait_ge(sem["l2"], L2_TOTAL)
                    for nt in range(N_QK):
                        t.wait_ge(sem["w2s%d" % (nt % 2)], 16 * (nt // 2 + 1))
                        for tc in range(4):
                            if nt >= 2:
                                t.wait_ge(sem["e2"], 4 * (nt - 2) + tc + 1)
                            for k in range(NK):
                                ins = t.matmul(qk_ps[nt % 2][:, tc * 512:(tc + 1) * 512],
                                               wqk_blk[nt % 2][:, k, :],
                                               xnT_sb[:, 4 * tc:4 * (tc + 1), k, :],
                                               start=(k == 0), stop=(k == NK - 1))
                            ins.then_inc(sem["m2"], 1)

                @block.vector
                def _(v):
                    for nt in range(N_QK):
                        for tc in range(4):
                            v.wait_ge(sem["m2"], 4 * nt + tc + 1)
                            if nt >= 2 and tc == 0:
                                v.wait_ge(sem["st2s%d" % (nt % 2)], 16 * (nt // 2))
                            bias = bq_sb[:, nt:nt + 1] if nt < 32 else bk_sb[:, nt - 32:nt - 31]
                            v.tensor_scalar(qk_stage[nt % 2][:, tc * 512:(tc + 1) * 512],
                                            qk_ps[nt % 2][:, tc * 512:(tc + 1) * 512],
                                            bias, None, ALU.add).then_inc(sem["e2"], 1)

                @block.sync
                def _(sync):
                    for nt in range(N_QK):
                        sync.wait_ge(sem["e2"], 4 * nt + 4)
                        dst = s_qT[nt * 128:(nt + 1) * 128, :] if nt < 32 else \
                            s_kT[(nt - 32) * 128:(nt - 31) * 128, :]
                        sync.dma_start(out=dst, in_=qk_stage[nt % 2][:]
                                       ).then_inc(sem["st2s%d" % (nt % 2)], 16)

            v_ps = [pc2(nc.psum_tensor(f"v_ps{i}", [128, 512], F32)) for i in range(4)]

            @block.tensor
            def _(t):
                t.wait_ge(sem["e2"], 4 * N_QK)
                for vb in range(NVB):
                    t.wait_ge(sem["w2vs%d" % (vb % 2)], 16 * (vb // 2 + 1))
                    for tt in range(NT):
                        j = vb * NT + tt
                        if j >= 4:
                            t.wait_ge(sem["e2v"], j - 3)
                        for k in range(NK):
                            ins = t.matmul(v_ps[j % 4][:, 0:256], xnT_sb[:, tt, k, :],
                                           wv_blk[vb % 2][:, k, :],
                                           start=(k == 0), stop=(k == NK - 1))
                        ins.then_inc(sem["m2v"], 1)

            @block.vector
            def _(v):
                for vb in range(NVB):
                    for tt in range(NT):
                        j = vb * NT + tt
                        v.wait_ge(sem["m2v"], j + 1)
                        if j >= 4:
                            v.wait_ge(sem["st2vs%d" % (j % 4)], 16 * (j // 4))
                        v.tensor_tensor(v_stage[j % 4][:], v_ps[j % 4][:, 0:256],
                                        bv_sb[:, vb * 256:(vb + 1) * 256], ALU.add).then_inc(sem["e2v"], 1)

            @block.sync
            def _(sync):
                for vb in range(NVB):
                    for tt in range(NT):
                        j = vb * NT + tt
                        sync.wait_ge(sem["e2v"], j + 1)
                        sync.dma_start(out=s_v[tt * 128:(tt + 1) * 128, vb * 256:(vb + 1) * 256],
                                       in_=v_stage[j % 4][:]).then_inc(sem["st2vs%d" % (j % 4)], 16)

        if phases < 3:
            drain(ST2_DONE + ST2V_DONE)
            return nc

        NJ = NFB * NT

        m4_sc, m4_pv, a4_exp, d4_mrg = {}, {}, {}, {}
        a4 = d4 = 0
        d4 += 2
        D4_SETUP = d4
        ei_of = {}
        work = []
        pairs = []
        for c in range(NCQ):
            ktl = 4 * (c + 1)
            for g in range(16):
                for hi in range(2):
                    pairs.append((c, g, hi))
                    for lt in range(ktl):
                        work.append((c, g, hi, lt))
                        a4 += 1
                        a4_exp[(c, g, hi, lt)] = a4
                        ei_of[(c, g, hi, lt)] = a4 - 1
                    d4 += 8
                    d4_mrg[(c, g, hi)] = d4
        pe_order = [("sc", 0)]
        for i in range(len(work)):
            if i + 1 < len(work):
                pe_order.append(("sc", i + 1))
            pe_order.append(("pv", i))
        m4 = 0
        for kind, i in pe_order:
            m4 += 1
            if kind == "sc":
                m4_sc[work[i]] = m4
            else:
                m4_pv[work[i]] = m4

        NI = len(pe_order)
        win = [(NI * t // NJ, NI * (t + 1) // NJ) for t in range(NJ)]
        tick_of_item = np.zeros(NI, dtype=int)
        for t, (a, b) in enumerate(win):
            tick_of_item[a:b] = t
        first_sc_item = {}
        last_sc_item = {}
        last_pv_item = {}
        for idx, (kind, i) in enumerate(pe_order):
            if kind == "sc":
                c, g, hi, lt = work[i]
                if lt == 0:
                    first_sc_item[(c * 16 + g) * 2 + hi] = idx
                last_sc_item[(c * 16 + g) * 2 + hi] = idx
            if kind == "pv":
                c, g, hi, lt = work[i]
                if lt == 4 * (c + 1) - 1:
                    last_pv_item[(c * 16 + g) * 2 + hi] = idx
        load_tick = {}
        for pi in range(len(pairs)):
            c, g, hi = pairs[pi]
            cg = c * 16 + g
            lt_ = int(tick_of_item[first_sc_item[pi]]) - 2
            if hi == 0 and cg >= 2:
                lt_ = max(lt_, int(tick_of_item[last_pv_item[2 * cg - 3]]) + 1)
            if pi >= 2:
                lt_ = max(lt_, int(tick_of_item[last_sc_item[pi - 2]]) + 1)
            lt_ = max(0, lt_)
            assert lt_ <= int(tick_of_item[first_sc_item[pi]]), (pi, lt_)
            load_tick[pi] = lt_
        mrg_tick = {pi: int(tick_of_item[last_pv_item[pi]]) for pi in range(len(pairs))}
        atst_tick = {pi: min(NJ - 1, mrg_tick[pi] + 1) for pi in range(len(pairs))}
        loads_at = [[] for _ in range(NJ)]
        mrg_at = [[] for _ in range(NJ)]
        atst_at = [[] for _ in range(NJ)]
        for pi in range(len(pairs)):
            loads_at[load_tick[pi]].append(pi)
            mrg_at[mrg_tick[pi]].append(pi)
            atst_at[atst_tick[pi]].append(pi)
        exps_at = [[] for _ in range(NJ)]
        for idx, (kind, i) in enumerate(pe_order):
            if kind == "sc":
                exps_at[int(tick_of_item[idx])].append(i)

        def st2_wait_upto(eng, nt_max):
            for par in range(2):
                cnt = sum(1 for n in range(nt_max + 1) if n % 2 == par)
                if cnt:
                    eng.wait_ge(sem["st2s%d" % par], 16 * cnt)

        with ExitStack() as p34:
            pc3 = p34.enter_context
            xqT_sb = pc3(nc.sbuf_tensor("xqT_sb", [128, NT, NK, 128], BF16))
            ws_ch = [pc3(nc.sbuf_tensor(f"wsch{i}", [128, 256], F32)) for i in range(2)]
            wf_blk = [pc3(nc.sbuf_tensor(f"wfb{i}", [128, NK, 256], BF16)) for i in range(2)]
            y16 = [pc3(nc.sbuf_tensor(f"y16_{i}", [128, 256], FP16)) for i in range(2)]
            x_strip = [pc3(nc.sbuf_tensor(f"xs3_{i}", [128, 256], F32)) for i in range(2)]
            par_stage = [pc3(nc.sbuf_tensor(f"par{i}", [128, 256], F32)) for i in range(2)]
            f_ps = [pc3(nc.psum_tensor(f"f_ps{i}", [128, 512], F32)) for i in range(2)]
            kt_sb = [pc3(nc.sbuf_tensor(f"kt{i}", [128, T], BF16)) for i in range(2)]
            v_sb = [pc3(nc.sbuf_tensor(f"v4_{i}", [128, NT, 144], BF16)) for i in range(2)]
            qt_sb = [pc3(nc.sbuf_tensor(f"qt{i}", [128, 512], BF16)) for i in range(2)]
            e_sb = [pc3(nc.sbuf_tensor(f"e{i}", [128, 512], BF16)) for i in range(2)]
            sums4 = pc3(nc.sbuf_tensor("sums4", [128, 4], F32))
            rec4 = pc3(nc.sbuf_tensor("rec4", [128, 4], F32))
            at_st = [pc3(nc.sbuf_tensor(f"at_st{i}", [128, 4, 128], BF16)) for i in range(2)]
            sc_ps = [pc3(nc.psum_tensor(f"sc_ps{i}", [128, 512], F32)) for i in range(2)]
            pv_ps = [pc3(nc.psum_tensor(f"pv_ps{i}", [128, 512], F32)) for i in range(4)]

            @block.gpsimd
            def _(g):
                wait_all(g, ST2_DONE + ST2V_DONE)
                for fb in range(NFB):
                    if fb >= 2:
                        g.wait_ge(sem["m3"], NT * (fb - 1))
                    g.dma_start(out=wf_blk[fb % 2][:],
                                in_=wf[fb].rearrange("p (kc m) -> p kc m", kc=NK)
                                ).then_inc(sem["w3s%d" % (fb % 2)], 16)

            def emit_p4_loads(sync, pi):
                c, g, hi = pairs[pi]
                cg = c * 16 + g
                ktl = 4 * (c + 1)
                kmax = 512 * (c + 1)
                if hi == 0:
                    if cg >= 2:
                        pc_, pg_ = (cg - 2) // 16, (cg - 2) % 16
                        sync.wait_ge(sem["m4"], m4_pv[(pc_, pg_, 1, 4 * (pc_ + 1) - 1)])
                    else:
                        st2_wait_upto(sync, 32 + g)
                        for sl in range(4):
                            sync.wait_ge(sem["st2vs%d" % sl], 64 * (g // 2 + 1))
                    sync.dma_start(out=kt_sb[g % 2][:, 0:kmax],
                                   in_=s_kT[g * 128:(g + 1) * 128, 0:kmax]
                                   ).then_inc(sem["kt4s%d" % (g % 2)], 16)
                    sync.dma_start(out=v_sb[g % 2][:, 0:ktl, 0:128],
                                   in_=s_v[0:kmax, g * 128:(g + 1) * 128]
                                   .rearrange("(n p) d -> p n d", p=128)
                                   ).then_inc(sem["v4s%d" % (g % 2)], 16)
                h = g + 16 * hi
                if pi >= 2:
                    pcg, phi = (pi - 2) // 2, (pi - 2) % 2
                    pc_, pg_ = pcg // 16, pcg % 16
                    sync.wait_ge(sem["m4"], m4_sc[(pc_, pg_, phi, 4 * (pc_ + 1) - 1)])
                else:
                    st2_wait_upto(sync, h)
                sync.dma_start(out=qt_sb[pi % 2][:],
                               in_=s_qT[h * 128:(h + 1) * 128, 512 * c:512 * (c + 1)]
                               ).then_inc(sem["qt4s%d" % (pi % 2)], 16)

            def emit_at_store(sync, pi):
                c, g, hi = pairs[pi]
                h = g + 16 * hi
                sync.wait_ge(sem["d4"], d4_mrg[(c, g, hi)])
                sync.dma_start(out=s_at[512 * c:512 * (c + 1), h * 128:(h + 1) * 128]
                               .rearrange("(n p) d -> p n d", p=128),
                               in_=at_st[pi % 2][:]).then_inc(sem["at4s%d" % (pi % 2)], 16)

            @block.sync
            def _(sync):
                wait_all(sync, ST1_DONE + ST2_DONE + ST2V_DONE)
                for q4 in range(4):
                    sync.dma_start(out=xqT_sb[:, q4 * 4:(q4 + 1) * 4, :, :],
                                   in_=xqT3[:, q4 * 4:(q4 + 1) * 4, :, :]).then_inc(sem["l3"], 16)
                for t in range(NJ):
                    fb, tt = t // NT, t % NT
                    j = t
                    if tt == 0:
                        if fb >= 2:
                            sync.wait_ge(sem["e3"], 2 * ((fb - 1) * NT - 1) + 1)
                        sync.dma_start(out=ws_ch[fb % 2][:],
                                       in_=ws_rep[:, fb * 256:(fb + 1) * 256]
                                       ).then_inc(sem["wsc%d" % (fb % 2)], 16)
                    if j >= 2:
                        sync.wait_ge(sem["e3"], 2 * (j - 2) + 2)
                    sync.dma_start(out=x_strip[j % 2][:],
                                   in_=x[tt * 128:(tt + 1) * 128, fb * 256:(fb + 1) * 256]
                                   ).then_inc(sem["x3s%d" % (j % 2)], 16)
                    if j >= 2:
                        jj = j - 2
                        fb2, tt2 = jj // NT, jj % NT
                        sync.wait_ge(sem["e3"], 2 * jj + 2)
                        sync.dma_start(out=s_par[tt2 * 128:(tt2 + 1) * 128, fb2 * 256:(fb2 + 1) * 256],
                                       in_=par_stage[jj % 2][:]).then_inc(sem["st3s%d" % (jj % 2)], 16)
                    for pi in loads_at[t]:
                        emit_p4_loads(sync, pi)
                    for pi in atst_at[t]:
                        emit_at_store(sync, pi)
                for jj in range(NJ - 2, NJ):
                    fb2, tt2 = jj // NT, jj % NT
                    sync.wait_ge(sem["e3"], 2 * jj + 2)
                    sync.dma_start(out=s_par[tt2 * 128:(tt2 + 1) * 128, fb2 * 256:(fb2 + 1) * 256],
                                   in_=par_stage[jj % 2][:]).then_inc(sem["st3s%d" % (jj % 2)], 16)
            L3_TOTAL = 4 * 16

            @block.tensor
            def _(t):
                def emit_scores2(i):
                    c, g, hi, lt = work[i]
                    cg = c * 16 + g
                    pi = cg * 2 + hi
                    if lt == 0:
                        t.wait_ge(sem["kt4s%d" % (g % 2)], 16 * (cg // 2 + 1))
                        t.wait_ge(sem["v4s%d" % (g % 2)], 16 * (cg // 2 + 1))
                        t.wait_ge(sem["qt4s%d" % (pi % 2)], 16 * (pi // 2 + 1))
                    if i >= 2:
                        t.wait_ge(sem["a4"], i - 1)
                    diag = lt >= 4 * c
                    ins = t.matmul(sc_ps[i % 2][:], kt_sb[g % 2][:, lt * 128:(lt + 1) * 128],
                                   qt_sb[pi % 2][:], start=True, stop=not diag)
                    if diag:
                        q0 = (lt - 4 * c) * 128
                        ins = t.matmul(sc_ps[i % 2][:, q0:q0 + 128], masku[:], ident[:],
                                       start=False, stop=True)
                    ins.then_inc(sem["m4"], 1)

                def emit_pv(i):
                    c, g, hi, lt = work[i]
                    pi = (c * 16 + g) * 2 + hi
                    t.wait_ge(sem["a4"], a4_exp[(c, g, hi, lt)])
                    if lt == 0:
                        if pi >= 1:
                            pcg, phi = (pi - 1) // 2, (pi - 1) % 2
                            t.wait_ge(sem["d4"], d4_mrg[(pcg // 16, pcg % 16, phi)])
                        else:
                            t.wait_ge(sem["d4"], D4_SETUP)
                    for qv in range(max(0, lt - 4 * c), 4):
                        ins = t.matmul(pv_ps[qv][:, 0:129],
                                       e_sb[(a4_exp[(c, g, hi, lt)] - 1) % 2][:, qv * 128:(qv + 1) * 128],
                                       v_sb[g % 2][:, lt, 0:129],
                                       start=(lt == 0), stop=(lt == 4 * c + qv))
                    ins.then_inc(sem["m4"], 1)

                t.wait_ge(sem["l3"], L3_TOTAL)
                t.wait_ge(sem["e2v"], NVB * NT)
                for t_ in range(NJ):
                    fb, tt = t_ // NT, t_ % NT
                    j = t_
                    t.wait_ge(sem["w3s%d" % (fb % 2)], 16 * (fb // 2 + 1))
                    if j >= 2:
                        t.wait_ge(sem["a3"], j - 1)
                    for k in range(NK):
                        ins = t.matmul(f_ps[j % 2][:, 0:256], xqT_sb[:, tt, k, :],
                                       wf_blk[fb % 2][:, k, :],
                                       start=(k == 0), stop=(k == NK - 1))
                    ins.then_inc(sem["m3"], 1)
                    a, b = win[t_]
                    for idx in range(a, b):
                        kind, i = pe_order[idx]
                        if kind == "sc":
                            emit_scores2(i)
                        else:
                            emit_pv(i)

            @block.scalar
            def _(s):
                for t_ in range(NJ):
                    for i in exps_at[t_]:
                        c, g, hi, lt = work[i]
                        ei = ei_of[(c, g, hi, lt)]
                        s.wait_ge(sem["m4"], m4_sc[(c, g, hi, lt)])
                        if ei >= 2:
                            s.wait_ge(sem["m4"], m4_pv[work[ei - 2]])
                        q0 = max(0, (lt - 4 * c) * 128)
                        s.activation(e_sb[ei % 2][:, q0:512], sc_ps[ei % 2][:, q0:512],
                                     AF.Exp, bias=cpool[:, 1:2],
                                     scale=cpool[:, 0:1]).then_inc(sem["a4"], 1)

            @block.vector
            def _(v):
                d4c = 2
                wait_all(v, ST2_DONE + ST2V_DONE)
                v.wait_ge(sem["mos"], 16)
                v.memset(v_sb[0][:, :, 128:129], 1.0).then_inc(sem["d4"], 1)
                v.memset(v_sb[1][:, :, 128:129], 1.0).then_inc(sem["d4"], 1)
                for t_ in range(NJ):
                    fb, tt = t_ // NT, t_ % NT
                    j = t_
                    v.wait_ge(sem["m3"], j + 1)
                    v.tensor_scalar(y16[j % 2][:], f_ps[j % 2][:, 0:256],
                                    mos_sb[:, 0:1], None, ALU.mult).then_inc(sem["a3"], 1)
                    v.wait_ge(sem["x3s%d" % (j % 2)], 16 * (j // 2 + 1))
                    v.wait_ge(sem["wsc%d" % (fb % 2)], 16 * (fb // 2 + 1))
                    if j >= 2:
                        v.wait_ge(sem["st3s%d" % (j % 2)], 16 * (j // 2))
                    v.scalar_tensor_tensor(par_stage[j % 2][:], y16[j % 2][:],
                                           xs_all[:, tt:tt + 1], ws_ch[fb % 2][:],
                                           ALU.mult, ALU.mult).then_inc(sem["e3"], 1)
                    v.tensor_tensor(par_stage[j % 2][:], par_stage[j % 2][:],
                                    x_strip[j % 2][:], ALU.add).then_inc(sem["e3"], 1)
                    for pi in mrg_at[t_]:
                        c, g, hi = pairs[pi]
                        ktl = 4 * (c + 1)
                        v.wait_ge(sem["m4"], m4_pv[(c, g, hi, ktl - 1)])
                        if pi >= 2:
                            v.wait_ge(sem["at4s%d" % (pi % 2)], 16 * (pi // 2))
                        for qv in range(4):
                            v.reciprocal(rec4[:, qv:qv + 1],
                                         pv_ps[qv][:, 128:129]).then_inc(sem["d4"], 1)
                            d4c += 1
                        v.wait_ge(sem["d4"], d4c)
                        for qv in range(4):
                            v.tensor_scalar(at_st[pi % 2][:, qv, :], pv_ps[qv][:, 0:128],
                                            rec4[:, qv:qv + 1], None, ALU.mult).then_inc(sem["d4"], 1)
                            d4c += 1
                        assert d4c == d4_mrg[(c, g, hi)]

        AT4_DONE = [("at4s%d" % jj, 16 * 64) for jj in range(2)]
        with ExitStack() as p5s:
            pc5 = p5s.enter_context
            p5_par = [pc5(nc.sbuf_tensor(f"p5p{i}", [128, D], F32)) for i in range(2)]
            p5_at = [pc5(nc.sbuf_tensor(f"p5a{i}", [128, D], BF16)) for i in range(2)]
            p5_out = [pc5(nc.sbuf_tensor(f"p5o{i}", [128, D], F32)) for i in range(2)]

            @block.sync
            def _(sync):
                wait_all(sync, ST3_DONE + AT4_DONE)
                for i in range(NT):
                    if i >= 2:
                        sync.wait_ge(sem["e5"], i - 1)
                    sync.dma_start(out=p5_par[i % 2][:],
                                   in_=s_par[i * 128:(i + 1) * 128, :]).then_inc(sem["p5a%d" % (i % 2)], 16)
                    sync.dma_start(out=p5_at[i % 2][:],
                                   in_=s_at[i * 128:(i + 1) * 128, :]).then_inc(sem["p5b%d" % (i % 2)], 16)
                    if i >= 2:
                        sync.dma_start(out=out[(i - 2) * 128:(i - 1) * 128, :],
                                       in_=p5_out[i % 2][:]).then_inc(sem["so"], 16)
                for i in range(NT - 2, NT):
                    sync.wait_ge(sem["e5"], i + 1)
                    sync.dma_start(out=out[i * 128:(i + 1) * 128, :],
                                   in_=p5_out[i % 2][:]).then_inc(sem["so"], 16)
                sync.wait_ge(sem["so"], 16 * NT)

            @block.vector
            def _(v):
                for i in range(NT):
                    v.wait_ge(sem["p5a%d" % (i % 2)], 16 * (i // 2 + 1))
                    v.wait_ge(sem["p5b%d" % (i % 2)], 16 * (i // 2 + 1))
                    if i >= 2:
                        v.wait_ge(sem["so"], 16 * (i - 1))
                    v.tensor_tensor(p5_out[i % 2][:], p5_par[i % 2][:],
                                    p5_at[i % 2][:], ALU.add).then_inc(sem["e5"], 1)

    return nc


_NC_CACHE = {}


def kernel(L0_attn_x, L0_attn_Wq, L0_attn_bq, L0_attn_Wk, L0_attn_bk,
           L0_attn_Wv, L0_attn_bv, L0_ffn_w_int8, L0_ffn_weight_scale,
           L0_ffn_matmul_out_scale):
    x = np.asarray(L0_attn_x)
    f32 = np.float32

    def img_qk(W, ntile):
        W = np.asarray(W, dtype=f32)
        return np.ascontiguousarray(
            W.reshape(ntile, 128, NK, 128).transpose(0, 3, 2, 1).reshape(ntile, 128, NK * 128))

    def img_256(W, nb, dt):
        W = np.asarray(W)
        return np.ascontiguousarray(
            W.reshape(nb, 256, NK, 128).transpose(0, 3, 2, 1).reshape(nb, 128, NK * 256)).astype(dt, copy=False)

    wq_img = img_qk(L0_attn_Wq, 32)
    wk_img = img_qk(L0_attn_Wk, 16)
    wv_img = img_256(L0_attn_Wv, NVB, f32)
    wf_img = img_256(L0_ffn_w_int8, NFB, np.int8)
    bq_img = np.ascontiguousarray(np.asarray(L0_attn_bq, dtype=f32).reshape(32, 128).T)
    bk_img = np.ascontiguousarray(np.asarray(L0_attn_bk, dtype=f32).reshape(16, 128).T)
    bv_img = np.ascontiguousarray(np.broadcast_to(np.asarray(L0_attn_bv, dtype=f32)[None, :], (128, KVD)))
    ws_img = np.ascontiguousarray(np.broadcast_to(np.asarray(L0_ffn_weight_scale, dtype=f32)[None, :], (128, D)))
    mos_img = np.ascontiguousarray(np.broadcast_to(np.asarray(L0_ffn_matmul_out_scale, dtype=f32).reshape(1, 1), (128, 1)))

    if "nc" not in _NC_CACHE:
        _NC_CACHE["nc"] = build_nc()
    nc = _NC_CACHE["nc"]

    in_maps = []
    for b in range(8):
        in_maps.append({
            "x": np.ascontiguousarray(x[b], dtype=f32),
            "wq": wq_img, "wk": wk_img, "wv": wv_img, "wf": wf_img,
            "bq": bq_img, "bk": bk_img, "bv_rep": bv_img,
            "ws_rep": ws_img, "mos_rep": mos_img,
        })
    res = run_bass_kernel_spmd(nc, in_maps, core_ids=list(range(8)))
    return np.stack([np.asarray(r["out"]) for r in res.results], axis=0)
